# version 13
# baseline (speedup 1.0000x reference)
import math
from contextlib import ExitStack

import numpy as np
import ml_dtypes

import concourse.bass as bass
import concourse.mybir as mybir
from concourse.bass_utils import run_bass_kernel_spmd

F32 = mybir.dt.float32
BF16 = mybir.dt.bfloat16
I32 = mybir.dt.int32
OP = mybir.AluOpType
AF = mybir.ActivationFunctionType
AX = mybir.AxisListType.X

S, D, H, HD, C = 2048, 1024, 16, 64, 4096
NST, NDC, NCC, NTC, NPAIR = 16, 8, 32, 16, 8
TWO_PI = 2.0 * math.pi
HSTRIDE = 528
LQ1, LQ2 = 2, 2

_DT = {F32: 4, BF16: 2, I32: 4}
LAYOUT = {}


class Plan:
    ENGINES = ("sync", "gpsimd", "tensor", "vector", "scalar")
    DMA_SEMS = {"LD", "GP", "TP", "OUT", "SP", "L2"}

    def __init__(self, nc, es):
        self.nc = nc
        self.q = {e: [] for e in self.ENGINES}
        self.sem = {}
        self.cnt = {}
        for name in ("LD", "GP", "TP", "OUT", "SP", "L2", "T", "V", "SC", "G1"):
            self.sem[name] = es.enter_context(nc.semaphore(name))
            self.cnt[name] = 0

    def step(self, eng, fn, waits=(), inc=None):
        h_w = [(self.sem[n], self.cnt[n] if n in self.DMA_SEMS else v)
               for (n, v) in waits if v and v > 0]
        if inc is None:
            def run(e, fn=fn, h_w=h_w):
                for h, v in h_w:
                    e.wait_ge(h, v)
                fn(e)
            self.q[eng].append(run)
            return None
        stepn = 16 if inc in self.DMA_SEMS else 1
        h_i = self.sem[inc]

        def run(e, fn=fn, h_w=h_w, h_i=h_i, stepn=stepn):
            for h, v in h_w:
                e.wait_ge(h, v)
            r = fn(e)
            if r is None:
                r = e.engine_nop()
            r.then_inc(h_i, stepn)
        self.q[eng].append(run)
        self.cnt[inc] += stepn
        return self.cnt[inc]

    def now(self, name):
        return self.cnt[name]

    def barrier(self):
        return [(n, self.cnt[n]) for n in ("T", "V", "SC", "TP", "SP", "LD", "GP", "OUT")
                if self.cnt[n] > 0]


class SubAlloc:

    def __init__(self, tensor, nbytes, esz):
        self.t = tensor
        self.nbytes = nbytes
        self.esz = esz

    def at(self, off, cols):
        assert off % 32 == 0 and off + cols * self.esz <= self.nbytes, (off, cols)
        return self.t[:, off // self.esz: off // self.esz + cols]


def build_program():
    nc = bass.Bass("TRN2", target_bir_lowering=False, debug=False)

    d_xfm = nc.dram_tensor("x_fm", [D, S], F32, kind="ExternalInput")
    d_xtm = nc.dram_tensor("x_tm", [S, D], F32, kind="ExternalInput")
    d_wq = nc.dram_tensor("wqT", [D, D], F32, kind="ExternalInput")
    d_wk = nc.dram_tensor("wkT", [D, D], F32, kind="ExternalInput")
    d_wv = nc.dram_tensor("wvT", [D, D], F32, kind="ExternalInput")
    d_wo = nc.dram_tensor("woutT", [D, D], F32, kind="ExternalInput")
    d_w1 = nc.dram_tensor("w1T", [D, C], F32, kind="ExternalInput")
    d_w2 = nc.dram_tensor("w2T", [C, D], F32, kind="ExternalInput")
    d_cw = nc.dram_tensor("convw", [128, 96], F32, kind="ExternalInput")
    d_cb = nc.dram_tensor("convb", [128, 32], F32, kind="ExternalInput")
    d_cos = nc.dram_tensor("costab", [128, NST * HD], F32, kind="ExternalInput")
    d_sin = nc.dram_tensor("sintab", [128, NST * HD], F32, kind="ExternalInput")
    d_ident = nc.dram_tensor("identb", [128, 128], BF16, kind="ExternalInput")
    d_identf = nc.dram_tensor("identf", [128, 128], F32, kind="ExternalInput")
    d_ones = nc.dram_tensor("onesf", [128, 128], F32, kind="ExternalInput")
    d_sel = nc.dram_tensor("sel8", [8, 512], BF16, kind="ExternalInput")
    d_consts = nc.dram_tensor("consts", [128, 8], F32, kind="ExternalInput")
    d_y = nc.dram_tensor("y", [S, D], F32, kind="ExternalOutput")
    d_x1sp = nc.dram_tensor("x1spill", [S, D], F32, kind="Internal")
    d_hsp = nc.dram_tensor("hspill", [NCC, 128, S], BF16, kind="Internal")

    es = ExitStack()
    with es:
        BF_BYTES = 2560 + 40960 + 99328 + 32768
        F_BYTES = 2112 + 24 * 1024
        I_BYTES = 64 + 4096
        abf = SubAlloc(es.enter_context(nc.sbuf_tensor("abf", [128, BF_BYTES // 2], BF16)),
                       BF_BYTES, 2)
        af = SubAlloc(es.enter_context(nc.sbuf_tensor("af", [128, F_BYTES // 4], F32)),
                      F_BYTES, 4)
        ai = SubAlloc(es.enter_context(nc.sbuf_tensor("ai", [128, I_BYTES // 4], I32)),
                      I_BYTES, 4)
        pl = Plan(nc, es)

        sb_ident = abf.at(0, 128)
        sb_sel = abf.at(256, 512)
        lbsb = abf.at(1280, 512)
        rlrow = abf.at(2304, 128)
        BF_A = 2560
        BF_B = BF_A + 40960
        BF_C = BF_B + 99328
        assert BF_C + 32768 == BF_BYTES
        x_fmb = abf.at(BF_A, NDC * S)
        ptm = [abf.at(BF_A, S), abf.at(BF_A + 4096, S)]
        PT = [abf.at(BF_A + 8192, NTC * 512), abf.at(BF_A + 24576, NTC * 512)]
        wo_b = abf.at(BF_A, NDC * D)
        xqtm = [abf.at(BF_A + 16384, D), abf.at(BF_A + 18432, D)]
        h2tm = abf.at(BF_A, 4 * C)
        xq2cm = abf.at(BF_A + 32768, NCC * 128)
        qa_w = abf.at(BF_A + 20480, 1024)
        qb_w = abf.at(BF_A + 22528, 1024)
        qd_w = abf.at(BF_A + 24576, 1024)
        q_fm = abf.at(BF_B, NDC * S)
        k_fm = abf.at(BF_B + 32768, NDC * S)
        v_tm = abf.at(BF_B + 65536, NST * D)
        xqg_fm = abf.at(BF_B, NDC * S)
        tern1 = abf.at(BF_B + 32768, NDC * C)
        tern2 = abf.at(BF_B, NCC * D)
        hsb = abf.at(BF_B + 65536, NCC * HSTRIDE)
        wgrp = abf.at(BF_C, NDC * D)
        rot_buf = [abf.at(BF_C + 16384, D), abf.at(BF_C + 18432, D)]
        attnout = abf.at(BF_C, NDC * S)
        hstage = [abf.at(BF_C, S), abf.at(BF_C + 4096, S)]
        diagb = [abf.at(BF_C, 128), abf.at(BF_C + 256, 128), abf.at(BF_C + 512, 128)]
        qa_c = abf.at(BF_C + 1024, 1024)
        qb_c = abf.at(BF_C + 3072, 1024)
        qd_c = abf.at(BF_C + 5120, 1024)

        sb_consts = af.at(0, 8)
        sb_ones = af.at(32, 128)
        sb_identf = af.at(544, 128)
        sb_cw = af.at(1056, 96)
        sb_cb = af.at(1440, 32)
        sb_vec = af.at(1568, 64)
        nmbuf = af.at(1824, 8)
        lbuf = af.at(1856, 8)
        rlb = af.at(1888, 8)
        g3v = af.at(1920, 4)
        mac1 = af.at(1952, 8)
        mac2 = af.at(1984, 32)
        F_W = 2112
        sb_cos = af.at(F_W, NST * 64)
        sb_sins = af.at(F_W + 4096, NST * 64)
        ropeA = af.at(F_W + 8192, D)
        cos_rep = af.at(F_W + 12288, D)
        sin_rep = af.at(F_W + 16384, D)
        wstage = af.at(F_W, 4096)
        scr = af.at(F_W, D)
        r1b = af.at(F_W + 4096, D)
        x1b = [af.at(F_W + 8192, D), af.at(F_W + 12288, D)]
        xst = af.at(F_W + 16384, D)
        r2b = af.at(F_W + 4096, D)
        outb = [af.at(F_W + 8192, D), af.at(F_W + 12288, D)]
        qscr = af.at(F_W + 20480, D)

        probe_i32 = ai.at(0, 1)
        ti32 = ai.at(64, 64)
        yi32 = ai.at(64, 1024)
        wscr_i32 = ai.at(64, 1024)
        u32 = ai.at(64, 1024)

        LAYOUT.update(BF_A=BF_A, BF_B=BF_B, BF_C=BF_C, F_W=F_W)

        PS_A = es.enter_context(nc.psum_tensor("psA", [128, 2048], F32))
        PS_B = es.enter_context(nc.psum_tensor("psB", [128, 2048], F32))
        ropeB = PS_A[:, 1024:2048]

        def chunk(t, n, i, a, b):
            return t.rearrange("p (n c) -> p n c", n=n)[:, i, a:b]

        def blocked(t, n, a, b):
            return t.rearrange("p (n c) -> p n c", n=n)[:, :, a:b]

        def vs(i):
            return sb_vec[:, i:i + 1]

        V_MW1, V_S1, V_MW2, V_S2, V_OVEC, V_O2 = 0, 1, 2, 3, 4, 5
        V_SCR = 8

        def dma(out, in_, waits=(), sem="LD", eng="sync"):
            return pl.step(eng, lambda e, o=out, i=in_: e.dma_start(out=o, in_=i),
                           waits=waits, inc=sem)

        def dmat(out, in_, waits=(), eng="sync"):
            return pl.step(eng, lambda e, o=out, i=in_: e.dma_start(out=o, in_=i, transpose=True),
                           waits=waits, inc="TP")

        for dst, src in ((sb_consts, d_consts.ap()), (sb_ident, d_ident.ap()),
                         (sb_identf, d_identf.ap()), (sb_ones, d_ones.ap()),
                         (sb_sel[0:8, :], d_sel.ap()), (sb_cw, d_cw.ap()),
                         (sb_cb, d_cb.ap()), (sb_cos, d_cos.ap()),
                         (sb_sins, d_sin.ap())):
            dma(dst, src)
        ld0 = pl.now("LD")

        def cast_load(dst_bf, src_dram, rows, waits=()):
            nch = rows // 128
            return dma(dst_bf.rearrange("p (n c) -> p n c", n=nch),
                       src_dram.ap().rearrange("(n p) c -> p n c", p=128),
                       waits=waits, sem="GP", eng="gpsimd")

        gp_xfm = cast_load(x_fmb, d_xfm, D)

        def qround(e, x_ap, scale_ap, yi, acca, accb, dbf, out, J):
            e.tensor_scalar(out=yi, in0=x_ap, scalar1=scale_ap, scalar2=vs(V_OVEC),
                            op0=OP.mult, op1=OP.add)
            e.drain()
            e.tensor_scalar_add(out=yi, in0=yi, scalar1=-512.0)
            e.drain()
            e.scalar_tensor_tensor(out=dbf, in0=x_ap, scalar=scale_ap, in1=yi,
                                   op0=OP.mult, op1=OP.subtract)
            e.drain()
            ths = [j - 0.5 for j in range(-J + 1, J + 1)]
            e.tensor_scalar(out=acca, in0=dbf, scalar1=ths[0], scalar2=None,
                            op0=OP.is_ge)
            if J > 1:
                e.tensor_scalar(out=accb, in0=dbf, scalar1=ths[1], scalar2=None,
                                op0=OP.is_ge)
            e.drain()
            for i in range(2, 2 * J, 2):
                e.scalar_tensor_tensor(out=acca, in0=dbf, scalar=ths[i], in1=acca,
                                       op0=OP.is_ge, op1=OP.add)
                e.scalar_tensor_tensor(out=accb, in0=dbf, scalar=ths[i + 1], in1=accb,
                                       op0=OP.is_ge, op1=OP.add)
                e.drain()
            if J > 1:
                e.tensor_tensor(out=acca, in0=acca, in1=accb, op=OP.add)
                e.drain()
            else:
                e.scalar_tensor_tensor(out=acca, in0=dbf, scalar=ths[1], in1=acca,
                                       op0=OP.is_ge, op1=OP.add)
                e.drain()
            e.scalar_tensor_tensor(out=out, in0=yi, scalar=float(-J), in1=acca,
                                   op0=OP.add, op1=OP.add)
            e.drain()

        pl.step("vector", lambda e: e.tensor_copy(out=probe_i32, in_=sb_consts[:, 0:1]),
                waits=[("LD", ld0)])
        pl.step("vector", lambda e: e.drain())
        v_tbl = pl.step("vector", lambda e: e.tensor_scalar(out=vs(V_OVEC), in0=probe_i32,
                                                            scalar1=-0.5, scalar2=512.5,
                                                            op0=OP.mult, op1=OP.add),
                        inc="V")

        t_last_g = 0
        last_rd = {0: ("V", 0), 1: ("V", 0)}
        rot_tp = 0
        for gi, wd in enumerate((d_wq, d_wk, d_wv)):
            g = "qkv"[gi]
            gp_w = cast_load(wgrp, wd, D, waits=[("T", t_last_g)])
            for st in range(NST):
                half = st % 2
                psg = (PS_A if half == 0 else PS_B)[:, 0:1024]
                w = [last_rd[half]]
                if st == 0:
                    w += [("GP", gp_w), ("V", v_tbl)]
                t_mm = None
                for K in range(NDC):
                    lhs = chunk(x_fmb, NDC, K, st * 128, st * 128 + 128)
                    for n in range(2):
                        rhs = chunk(wgrp, NDC, K, n * 512, (n + 1) * 512)
                        last = (K == NDC - 1 and n == 1)
                        t_mm = pl.step("tensor", lambda e, o=psg[:, n * 512:(n + 1) * 512],
                                       l=lhs, r=rhs, s0=(K == 0), s1=(K == NDC - 1):
                                       e.matmul(o, l, r, start=s0, stop=s1),
                                       waits=w if (K == 0 and n == 0) else (),
                                       inc="T" if last else None)
                if g in ("q", "k"):
                    def reps(e, st=st):
                        e.drain()
                        for tbl, rep in ((sb_cos, cos_rep), (sb_sins, sin_rep)):
                            e.tensor_copy(out=rep[:, 0:64], in_=tbl[:, st * 64:st * 64 + 64])
                        e.drain()
                        for rep in (cos_rep, sin_rep):
                            for hh in range(1, 16):
                                e.tensor_copy(out=rep[:, hh * 64:hh * 64 + 64],
                                              in_=rep[:, 0:64])
                        e.drain()
                    pl.step("vector", reps, waits=[("T", t_mm)])
                    rb = rot_buf[st % 2]

                    def rope(e, psg=psg, rb=rb):
                        e.tensor_tensor(out=ropeA[:], in0=psg, in1=cos_rep[:], op=OP.mult)
                        sw = psg.rearrange("p (a b) -> p a b", b=2)[:, :, ::-1]
                        e.tensor_tensor(out=ropeB.rearrange("p (a b) -> p a b", b=2),
                                        in0=sw, in1=sin_rep[:].rearrange("p (a b) -> p a b", b=2),
                                        op=OP.mult)
                        e.drain()
                        e.tensor_tensor(out=rb, in0=ropeA[:], in1=ropeB, op=OP.add)
                    v_r = pl.step("vector", rope, waits=[("TP", rot_tp)], inc="V")
                    rot_tp = 1
                    last_rd[half] = ("V", v_r)
                    dstt = q_fm if g == "q" else k_fm
                    dmat(blocked(dstt, NDC, st * 128, st * 128 + 128), rb,
                         waits=[("V", v_r)])
                else:
                    last_rd[half] = ("SC", pl.step(
                        "scalar", lambda e, psg=psg, st=st: e.activation(
                            out=v_tm[:, st * 1024:(st + 1) * 1024], in_=psg,
                            func=AF.Copy),
                        waits=[("T", t_mm)], inc="SC"))
            t_last_g = pl.now("T")
        tp_p1 = pl.now("TP")
        sc_p1_end = pl.now("SC")

        t_p1_end = pl.now("T")
        wmchunk = [af.at(F_W + 4096 * i, 1024) for i in range(5)]
        macw = af.at(F_W + 20480, 128)
        _p15_srcs = []
        for wd, nch, cwid in ((d_w1, NDC, C), (d_w2, NCC, D)):
            for ch in range(nch):
                for q in range(cwid // 1024):
                    _p15_srcs.append(wd.ap()[ch * 128:(ch + 1) * 128,
                                             q * 1024:(q + 1) * 1024])
        n_p15 = len(_p15_srcs)
        _p15_state = {"nd": 0, "na": 0, "ldn": {}, "sc": {}}

        def p15_pump(flush=False):
            st_ = _p15_state
            while st_["na"] < n_p15:
                k, m = st_["nd"], st_["na"]
                if k < n_p15 and k - m <= 4:
                    if k == 0:
                        w = [("T", t_p1_end), ("TP", tp_p1)]
                    elif k >= 5:
                        w = [("V", st_["sc"][k - 5])]
                    else:
                        w = ()
                    st_["ldn"][k] = dma(wmchunk[k % 5][:], _p15_srcs[k], waits=w,
                                        sem="L2", eng="scalar")
                    st_["nd"] = k + 1
                elif m < k:
                    st_["sc"][m] = pl.step(
                        "vector", lambda e, m=m: e.reduce_sum(
                            out=macw[:, m:m + 1], in_=wmchunk[m % 5][:], axis=AX,
                            apply_absolute_value=True),
                        waits=[("L2", st_["ldn"][m])], inc="V")
                    st_["na"] = m + 1
                if not flush:
                    break

        def emit_means_finale(psum_free_v):
            p15_pump(flush=True)
            pl.step("vector", lambda e: e.drain())
            pl.step("vector", lambda e: e.reduce_sum(out=vs(V_SCR), in_=macw[:, 0:32],
                                                     axis=AX))
            v_mac = pl.step("vector", lambda e: e.reduce_sum(out=vs(V_SCR + 1),
                                                             in_=macw[:, 32:64],
                                                             axis=AX), inc="V")
            for wi in range(2):
                tot = PS_A[0:1, wi:wi + 1]
                t_tot = pl.step("tensor", lambda e, wi=wi, tot=tot: e.matmul(
                    tot, sb_ones[:, 0:1], vs(V_SCR + wi), start=True, stop=True),
                    waits=[("V", max(v_mac, psum_free_v))], inc="T")

                def mseq(e, tot=tot):
                    e.drain()
                    e.tensor_scalar(out=vs(V_SCR + 2)[0:1, :], in0=tot, scalar1=1.0 / (D * C),
                                    scalar2=1e-5, op0=OP.mult, op1=OP.max)
                    e.drain()
                    e.reciprocal(out=vs(V_SCR + 3)[0:1, :], in_=vs(V_SCR + 2)[0:1, :])
                    e.drain()
                v_m = pl.step("vector", mseq, waits=[("T", t_tot)], inc="V")
                pl.step("tensor", lambda e: e.matmul(PS_B[:, 0:1], sb_ones[0:1, 0:128],
                                                     vs(V_SCR + 2)[0:1, :], start=True,
                                                     stop=True),
                        waits=[("V", v_m)])
                t_b = pl.step("tensor", lambda e: e.matmul(
                    PS_B[:, 1:2], sb_ones[0:1, 0:128], vs(V_SCR + 3)[0:1, :],
                    start=True, stop=True), inc="T")
                mw, sc_ = (V_MW1, V_S1) if wi == 0 else (V_MW2, V_S2)
                pl.step("vector", lambda e, mw=mw: e.tensor_copy(out=vs(mw), in_=PS_B[:, 0:1]),
                        waits=[("T", t_b)])
                pl.step("vector", lambda e, sc_=sc_: e.tensor_copy(out=vs(sc_),
                                                                   in_=PS_B[:, 1:2]), inc="V")

        v_p1_end = pl.now("V")
        exp_done = {0: 0, 1: 0}
        ptm_tp = {0: 0, 1: 0}
        pt_rd = {0: 0, 1: 0}
        v_evac = 0
        first = True
        for p in range(NPAIR):
            for scn in range(4):
                exp_last = {}
                tp_last = {0: 0, 1: 0}
                for j in range(4):
                    st = scn * 4 + j
                    t_sc = {}
                    v_nm = {}
                    for hb in range(2):
                        PSX = PS_A if hb == 0 else PS_B
                        rows = slice(hb * 64, hb * 64 + 64)
                        lhs = q_fm[rows, p * 2048 + st * 128: p * 2048 + st * 128 + 128]
                        w = [("SC", exp_done[hb]), ("V", v_evac)]
                        if p == 0 and scn == 0 and j == 0:
                            w += [("TP", tp_p1), ("V", v_p1_end), ("SC", sc_p1_end)]
                        p15_pump()
                        for n in range(4):
                            rhs = k_fm[rows, p * 2048 + n * 512: p * 2048 + (n + 1) * 512]
                            t_sc[hb] = pl.step(
                                "tensor", lambda e, o=PSX[:, n * 512:(n + 1) * 512],
                                l=lhs, r=rhs: e.matmul(o, l, r, start=True, stop=True),
                                waits=w if n == 0 else (),
                                inc="T" if n == 3 else None)
                    for hb in range(2):
                        PSX = PS_A if hb == 0 else PS_B
                        col = hb * 4 + j
                        v_nm[hb] = pl.step("vector", lambda e, PSX=PSX, col=col: e.reduce_max(
                            out=nmbuf[:, col:col + 1], in_=PSX[:], axis=AX, negate=True),
                            waits=[("T", t_sc[hb])], inc="V")
                    for hb in range(2):
                        PSX = PS_A if hb == 0 else PS_B
                        col = hb * 4 + j
                        pb = ptm[hb]
                        sc_exp = pl.step("scalar", lambda e, PSX=PSX, pb=pb, col=col:
                                         e.activation(out=pb, in_=PSX[:], func=AF.Exp,
                                                      bias=nmbuf[:, col:col + 1], scale=1.0,
                                                      accum_out=lbuf[:, col:col + 1]),
                                         waits=[("V", v_nm[hb]),
                                                ("TP", ptm_tp[hb])], inc="SC")
                        exp_done[hb] = sc_exp
                        exp_last[(j, hb)] = sc_exp
                        dst = blocked(PT[hb], NTC, j * 128, j * 128 + 128)
                        tpn = dmat(dst, pb, waits=[("SC", sc_exp), ("T", pt_rd[hb])])
                        ptm_tp[hb] = tpn
                        tp_last[hb] = tpn

                def rl(e):
                    e.reciprocal(out=rlb[:], in_=lbuf[:])
                v_rl = pl.step("vector", rl, waits=[("SC", exp_last[(3, 1)])], inc="V")
                rl_ps = PS_A[0:8, 1024:1152]
                t_rlt = pl.step("tensor", lambda e: e.transpose(rl_ps, rlb[:], sb_identf),
                                waits=[("V", v_rl), ("SC", exp_done[0])], inc="T")
                v_rlrow = pl.step("vector", lambda e: e.tensor_copy(out=rlrow[0:8, :],
                                                                    in_=rl_ps),
                                  waits=[("T", t_rlt)], inc="V")
                for hb in range(2):
                    PSX = PS_A if hb == 0 else PS_B
                    pvs = PSX[hb * 64:hb * 64 + 64, 0:512]
                    lbs = PSX[hb * 64:hb * 64 + 64, 512:1024]
                    h = 2 * p + hb
                    w = [("TP", tp_last[hb]), ("V", v_rlrow), ("SC", exp_done[hb])]
                    for tc in range(NTC):
                        lhsv = v_tm[:, tc * 1024 + h * 64: tc * 1024 + h * 64 + 64]
                        rhsp = PT[hb][:, tc * 512:(tc + 1) * 512]
                        pl.step("tensor", lambda e, o=pvs, l=lhsv, r=rhsp, s0=(tc == 0),
                                s1=(tc == NTC - 1): e.matmul(o, l, r, start=s0, stop=s1),
                                waits=w if tc == 0 else ())
                    t_pv = None
                    for jj in range(4):
                        sel = sb_sel[0:8, (hb * 4 + jj) * 64:(hb * 4 + jj) * 64 + 64]
                        t_pv = pl.step("tensor",
                                       lambda e, o=lbs[:, jj * 128:(jj + 1) * 128], sel=sel:
                                       e.matmul(o, sel, rlrow[0:8, :], start=True, stop=True),
                                       inc="T" if jj == 3 else None)
                    pt_rd[hb] = t_pv
                    dst = attnout[hb * 64:hb * 64 + 64,
                                  p * 2048 + scn * 512: p * 2048 + (scn + 1) * 512]
                    lrows = lbsb[hb * 64:hb * 64 + 64, :]

                    def pv_evac(e, dst=dst, pvs=pvs, lbs=lbs, lrows=lrows):
                        e.tensor_copy(out=lrows, in_=lbs)
                        e.drain()
                        e.tensor_tensor(out=dst, in0=pvs, in1=lrows, op=OP.mult)
                    v_evac = pl.step("vector", pv_evac, waits=[("T", t_pv)], inc="V")
                exp_done = {0: 0, 1: 0}
        emit_means_finale(pl.now("V"))
        v_attn = pl.now("V")

        gp_wo = cast_load(wo_b, d_wo, D, waits=pl.barrier())
        sp_x1 = {}
        v_lnst = {}
        v_q1d = {}
        xq_tp = {}
        x_free_v = 0
        for st in range(NST):
            psw = PS_A[:, 0:1024] if st % 2 == 0 else PS_A[:, 1024:2048]
            x_ld = dma(xst, d_xtm.ap()[st * 128:(st + 1) * 128, :],
                       waits=[("V", x_free_v if st else v_attn)])
            w = [("V", v_lnst.get(st - 2, 0))]
            if st == 0:
                w += [("GP", gp_wo), ("V", v_attn)]
            t_wo = None
            for K in range(NDC):
                lhs = attnout[:, K * 2048 + st * 128: K * 2048 + st * 128 + 128]
                for n in range(2):
                    last = (K == NDC - 1 and n == 1)
                    rhs = chunk(wo_b, NDC, K, n * 512, (n + 1) * 512)
                    t_wo = pl.step("tensor", lambda e, o=psw[:, n * 512:(n + 1) * 512], l=lhs,
                                   r=rhs, s0=(K == 0), s1=(K == NDC - 1):
                                   e.matmul(o, l, r, start=s0, stop=s1),
                                   waits=w if (K == 0 and n == 0) else (),
                                   inc="T" if last else None)

            def lnst(e, psw=psw):
                e.tensor_tensor(out=r1b[:], in0=psw, in1=xst, op=OP.add)
                e.drain()
                e.reduce_sum(out=vs(V_SCR), in_=r1b[:], axis=AX)
                e.drain()
                e.tensor_scalar_mul(out=vs(V_SCR + 1), in0=vs(V_SCR), scalar1=-1.0 / D)
                e.drain()
                e.scalar_tensor_tensor(out=scr[:], in0=r1b[:], scalar=vs(V_SCR + 1),
                                       in1=r1b[:], op0=OP.add, op1=OP.mult,
                                       accum_out=vs(V_SCR + 2))
            v_st = pl.step("vector", lnst, waits=[("T", t_wo), ("LD", x_ld)], inc="V")
            v_lnst[st] = v_st
            x_free_v = v_st
            sc_sq = pl.step("scalar", lambda e: e.activation(
                out=vs(V_SCR + 3), in_=vs(V_SCR + 2), func=AF.Sqrt,
                bias=sb_consts[:, 1:2], scale=sb_consts[:, 2:3]),
                waits=[("V", v_st)], inc="SC")

            def rstd(e):
                e.reciprocal(out=vs(V_SCR + 4), in_=vs(V_SCR + 3))
                e.drain()
                e.tensor_mul(out=vs(V_SCR + 5), in0=vs(V_SCR + 1), in1=vs(V_SCR + 4))
            v_rs = pl.step("vector", rstd, waits=[("SC", sc_sq)], inc="V")
            x1t = x1b[st % 2]

            def ln1apply(e, x1t=x1t):
                e.drain()
                e.tensor_scalar(out=x1t[:], in0=r1b[:], scalar1=vs(V_SCR + 4),
                                scalar2=vs(V_SCR + 5), op0=OP.mult, op1=OP.add)
            v_x1 = pl.step("vector", ln1apply,
                           waits=[("SP", sp_x1.get(st - 2, 0)),
                                  ("V", v_q1d.get(st - 2, 0))], inc="V")
            sp_x1[st] = dma(d_x1sp.ap()[st * 128:(st + 1) * 128, :], x1t[:],
                            waits=[("V", v_x1)], sem="SP", eng="gpsimd")
            xq = xqtm[st % 2]

            def q1(e, x1t=x1t, xq=xq):
                e.drain()
                e.reduce_max(out=vs(V_SCR), in_=x1t[:], axis=AX, apply_absolute_value=True)
                e.drain()
                e.tensor_scalar_max(out=vs(V_SCR + 1), in0=vs(V_SCR), scalar1=1e-5)
                e.drain()
                e.tensor_scalar_mul(out=vs(V_SCR + 2), in0=vs(V_SCR + 1), scalar1=1.0 / 127.0)
                e.tensor_scalar(out=vs(V_SCR + 4), in0=vs(V_SCR + 1), scalar1=vs(V_MW1),
                                scalar2=1.0 / 127.0, op0=OP.mult, op1=OP.mult)
                e.drain()
                e.reciprocal(out=vs(V_SCR + 3), in_=vs(V_SCR + 2))
                e.drain()
                qround(e, x1t[:], vs(V_SCR + 3), yi32[:], qa_w[:], qb_w[:],
                       qd_w[:], qscr[:], LQ1)
                e.tensor_scalar_mul(out=xq, in0=qscr[:], scalar1=vs(V_SCR + 4))
            v_q1 = pl.step("vector", q1, waits=[("TP", xq_tp.get(st - 2, 0))], inc="V")
            v_q1d[st] = v_q1
            xq_tp[st] = dmat(blocked(xqg_fm, NDC, st * 128, st * 128 + 128), xq,
                             waits=[("V", v_q1)])
        tp_xq = pl.now("TP")

        def ternarize(wd, nch, cw, tern, scslot, waits0):
            v_done = 0
            for ch in range(nch):
                w = list(waits0) if ch == 0 else []
                w.append(("V", v_done))
                ldn = dma(wstage[:, 0:cw], wd.ap()[ch * 128:(ch + 1) * 128, :], waits=w)
                dstt = chunk(tern, nch, ch, 0, cw)

                def tn(e, cw=cw, dstt=dstt, scslot=scslot):
                    e.drain()
                    for q0 in range(0, cw, 1024):
                        dch = dstt.rearrange("p (n c) -> p n c", n=cw // 1024)[:, q0 // 1024, :]
                        qround(e, wstage[:, q0:q0 + 1024], vs(scslot), wscr_i32[:],
                               qa_w[:], qb_w[:], qd_w[:], xst, 1)
                        e.tensor_scalar(out=dch, in0=xst, scalar1=1.0,
                                        scalar2=-1.0, op0=OP.min, op1=OP.max)
                        e.drain()
                v_done = pl.step("vector", tn, waits=[("LD", ldn)], inc="V")
            return v_done

        v_tern1 = ternarize(d_w1, NDC, C, tern1, V_S1, pl.barrier())

        v_hev = {}
        sp_h = {}
        for cc in range(NCC):
            psh = PS_A if cc % 2 == 0 else PS_B
            w = [("SC", v_hev.get(cc - 2, 0))]
            if cc == 0:
                w += [("V", v_tern1), ("TP", tp_xq)]
            t_h = None
            for K in range(NDC):
                lhs = chunk(tern1, NDC, K, cc * 128, cc * 128 + 128)
                for n in range(4):
                    last = (K == NDC - 1 and n == 3)
                    rhs = chunk(xqg_fm, NDC, K, n * 512, (n + 1) * 512)
                    t_h = pl.step("tensor", lambda e, o=psh[:, n * 512:(n + 1) * 512], l=lhs,
                                  r=rhs, s0=(K == 0), s1=(K == NDC - 1):
                                  e.matmul(o, l, r, start=s0, stop=s1),
                                  waits=w if (K == 0 and n == 0) else (),
                                  inc="T" if last else None)
            hb = hstage[cc % 2]
            v_hev[cc] = pl.step("scalar", lambda e, psh=psh, hb=hb: e.activation(
                out=hb, in_=psh[:], func=AF.Copy),
                waits=[("T", t_h), ("SP", sp_h.get(cc - 2, 0))], inc="SC")
            sp_h[cc] = dma(d_hsp.ap()[cc, :, :], hb, waits=[("SC", v_hev[cc])],
                           sem="SP", eng="gpsimd")

        v_tern2 = ternarize(d_w2, NCC, D, tern2, V_S2, pl.barrier())

        out_dma = {}
        t_conv_last = 0
        sc_silu_last = 0
        v_q2_last = 0
        t_w2_last = 0
        v_ev2 = {}
        for scn in range(4):
            lo, hi = scn * 512 - 1, scn * 512 + 513
            slo, shi = max(lo, 0), min(hi, S)
            doff = slo - lo
            src = d_hsp.ap()[:, :, slo:shi].transpose([1, 0, 2])
            dst = hsb.rearrange("p (n c) -> p n c", c=HSTRIDE)[:, :, doff:doff + (shi - slo)]
            w = [("T", t_conv_last), ("TP", pl.now("TP"))]
            if scn == 0:
                w = pl.barrier()
            ld_h = dma(dst, src, waits=w)

            def edges(e, scn=scn):
                hv = hsb.rearrange("p (n c) -> p n c", c=HSTRIDE)
                if scn == 0:
                    e.memset(hv[:, :, 0:1], 0.0)
                if scn == 3:
                    e.memset(hv[:, :, 513:514], 0.0)
                e.memset(hv[:, :, 514:HSTRIDE], 0.0)
            v_hld = pl.step("vector", edges, waits=[("LD", ld_h)], inc="V")
            sc_silu = {}
            for cc in range(NCC):
                psc = PS_A[:, (cc % 4) * 512:((cc % 4) + 1) * 512]

                def dg(e, cc=cc):
                    e.tensor_scalar_mul(out=diagb[0], in0=sb_ident,
                                        scalar1=sb_cw[:, cc * 3:cc * 3 + 1])
                    e.tensor_scalar_mul(out=diagb[1], in0=sb_ident,
                                        scalar1=sb_cw[:, cc * 3 + 1:cc * 3 + 2])
                    e.tensor_scalar_mul(out=diagb[2], in0=sb_ident,
                                        scalar1=sb_cw[:, cc * 3 + 2:cc * 3 + 3])
                v_d = pl.step("vector", dg, waits=[("V", v_hld), ("T", t_conv_last)], inc="V")
                w = [("V", v_d), ("SC", sc_silu.get(cc - 4, sc_silu_last))]
                t_cv = None
                for k in range(3):
                    rhs = hsb.rearrange("p (n c) -> p n c", c=HSTRIDE)[:, cc, k:k + 512]
                    t_cv = pl.step("tensor", lambda e, o=psc, dk=diagb[k], r=rhs, s0=(k == 0),
                                   s1=(k == 2): e.matmul(o, dk, r, start=s0, stop=s1),
                                   waits=w if k == 0 else (),
                                   inc="T" if k == 2 else None)
                t_conv_last = t_cv
                silu_dst = hsb[:, cc * HSTRIDE:cc * HSTRIDE + 512]
                sc_silu[cc] = pl.step("scalar", lambda e, psc=psc, sd=silu_dst, cc=cc:
                                      e.activation(out=sd, in_=psc, func=AF.Silu,
                                                   bias=sb_cb[:, cc:cc + 1], scale=1.0),
                                      waits=[("T", t_cv)], inc="SC")
            sc_silu_last = sc_silu[NCC - 1]
            tp_h2 = 0
            for cc in range(NCC):
                silu_src = hsb[:, cc * HSTRIDE:cc * HSTRIDE + 512]
                dst = blocked(h2tm, 4, cc * 128, (cc + 1) * 128)
                tp_h2 = dmat(dst, silu_src,
                             waits=[("SC", sc_silu[cc]), ("V", v_q2_last)])
            tp_bk = {}
            for j in range(4):
                st = scn * 4 + j
                h2 = h2tm[:, j * C:(j + 1) * C]

                def q2(e, h2=h2, j=j):
                    e.drain()
                    e.reduce_max(out=vs(V_SCR), in_=h2, axis=AX, apply_absolute_value=True)
                    e.drain()
                    e.tensor_scalar_max(out=vs(V_SCR + 1), in0=vs(V_SCR), scalar1=1e-5)
                    e.drain()
                    e.tensor_scalar_mul(out=vs(V_SCR + 2), in0=vs(V_SCR + 1), scalar1=1.0 / 127.0)
                    e.tensor_scalar(out=g3v[:, j:j + 1], in0=vs(V_SCR + 1), scalar1=vs(V_MW2),
                                    scalar2=1.0 / 127.0, op0=OP.mult, op1=OP.mult)
                    e.drain()
                    e.reciprocal(out=vs(V_SCR + 3), in_=vs(V_SCR + 2))
                    e.drain()
                    for q0 in range(0, C, 1024):
                        qround(e, h2[:, q0:q0 + 1024], vs(V_SCR + 3), u32[:],
                               qa_c[:], qb_c[:], qd_c[:], h2[:, q0:q0 + 1024], LQ2)
                v_q2 = pl.step("vector", q2, waits=[("TP", tp_h2)], inc="V")
                v_q2_last = v_q2
                tp_bk[j] = dmat(blocked(xq2cm, NCC, 0, 128), h2,
                                waits=[("V", v_q2), ("T", t_w2_last)])
                psw = PS_B[:, 0:1024] if j % 2 == 0 else PS_B[:, 1024:2048]
                w = [("TP", tp_bk[j]), ("V", v_ev2.get((scn, j - 2), 0))]
                if scn > 0 and j < 2:
                    w.append(("V", v_ev2.get((scn - 1, j + 2), 0)))
                t_w2 = None
                for cc in range(NCC):
                    lhs = chunk(xq2cm, NCC, cc, 0, 128)
                    for n in range(2):
                        last = (cc == NCC - 1 and n == 1)
                        rhs = chunk(tern2, NCC, cc, n * 512, (n + 1) * 512)
                        t_w2 = pl.step("tensor", lambda e, o=psw[:, n * 512:(n + 1) * 512],
                                       l=lhs, r=rhs, s0=(cc == 0), s1=(cc == NCC - 1):
                                       e.matmul(o, l, r, start=s0, stop=s1),
                                       waits=w if (cc == 0 and n == 0) else (),
                                       inc="T" if last else None)
                t_w2_last = t_w2
                ld_x1 = dma(xst, d_x1sp.ap()[st * 128:(st + 1) * 128, :],
                            waits=[("V", v_ev2.get((scn, j - 1),
                                                   v_ev2.get((scn - 1, 3), 0)) if (j or scn)
                                    else v_q2)])

                def ev2(e, psw=psw, j=j):
                    e.drain()
                    e.scalar_tensor_tensor(out=r2b[:], in0=psw, scalar=g3v[:, j:j + 1],
                                           in1=xst, op0=OP.mult, op1=OP.add)
                    e.drain()
                    e.reduce_sum(out=vs(V_SCR), in_=r2b[:], axis=AX)
                    e.drain()
                    e.tensor_scalar_mul(out=vs(V_SCR + 1), in0=vs(V_SCR), scalar1=-1.0 / D)
                    e.drain()
                    e.scalar_tensor_tensor(out=scr[:], in0=r2b[:], scalar=vs(V_SCR + 1),
                                           in1=r2b[:], op0=OP.add, op1=OP.mult,
                                           accum_out=vs(V_SCR + 2))
                v_s2 = pl.step("vector", ev2, waits=[("T", t_w2), ("LD", ld_x1)], inc="V")
                v_ev2[(scn, j)] = v_s2
                sc_sq2 = pl.step("scalar", lambda e: e.activation(
                    out=vs(V_SCR + 3), in_=vs(V_SCR + 2), func=AF.Sqrt,
                    bias=sb_consts[:, 1:2], scale=sb_consts[:, 2:3]),
                    waits=[("V", v_s2)], inc="SC")

                def rstd2(e):
                    e.reciprocal(out=vs(V_SCR + 4), in_=vs(V_SCR + 3))
                    e.drain()
                    e.tensor_mul(out=vs(V_SCR + 5), in0=vs(V_SCR + 1), in1=vs(V_SCR + 4))
                v_r2 = pl.step("vector", rstd2, waits=[("SC", sc_sq2)], inc="V")
                ob = outb[j % 2]

                def ln2apply(e, ob=ob):
                    e.drain()
                    e.tensor_scalar(out=ob[:], in0=r2b[:], scalar1=vs(V_SCR + 4),
                                    scalar2=vs(V_SCR + 5), op0=OP.mult, op1=OP.add)
                sc_out = pl.step("vector", ln2apply,
                                 waits=[("OUT", out_dma.get(st - 2, 0))], inc="V")
                out_dma[st] = dma(d_y.ap()[st * 128:(st + 1) * 128, :], ob[:],
                                  waits=[("V", sc_out)], sem="OUT")

        pl.step("sync", lambda e: e.nop(), waits=[("OUT", pl.now("OUT"))])
        pl.step("gpsimd", lambda e: e.nop(), waits=[("SP", pl.now("SP"))])

        with nc.Block() as blk:
            @blk.sync
            def _(e):
                for fn in pl.q["sync"]:
                    fn(e)

            @blk.gpsimd
            def _(e):
                for fn in pl.q["gpsimd"]:
                    fn(e)

            @blk.tensor
            def _(e):
                for fn in pl.q["tensor"]:
                    fn(e)

            @blk.vector
            def _(e):
                for fn in pl.q["vector"]:
                    fn(e)

            @blk.scalar
            def _(e):
                for fn in pl.q["scalar"]:
                    fn(e)

    return nc


def _rope_tables():
    import jax
    import jax.numpy as jnp
    with jax.default_device(jax.devices("cpu")[0]):
        pos = jnp.arange(S, dtype=jnp.float32)[:, None]
        ids = jnp.arange(HD // 2, dtype=jnp.float32)
        theta = jnp.power(jnp.float32(10000.0), 2.0 * ids / HD)
        ang = pos / theta
        sn = np.asarray(jnp.sin(ang), np.float32)
        cs = np.asarray(jnp.cos(ang), np.float32)
    cosr = np.repeat(cs, 2, axis=1)
    sinr = np.repeat(sn, 2, axis=1) * np.tile(
        np.array([-1.0, 1.0], np.float32), HD // 2)
    tab = lambda a: np.ascontiguousarray(
        a.reshape(NST, 128, HD).transpose(1, 0, 2).reshape(128, NST * HD))
    return tab(cosr), tab(sinr)


def _host_inputs(L0_attn_x, L0_attn_w_qkv, L0_attn_w_out, L0_ffn_w1,
                 L0_ffn_conv_w, L0_ffn_conv_b, L0_ffn_w2):
    costab, sintab = _rope_tables()
    x = np.ascontiguousarray(np.asarray(L0_attn_x, np.float32))
    wqkv = np.asarray(L0_attn_w_qkv, np.float32)
    wout = np.asarray(L0_attn_w_out, np.float32)
    w1 = np.asarray(L0_ffn_w1, np.float32)
    w2 = np.asarray(L0_ffn_w2, np.float32)
    cw = np.asarray(L0_ffn_conv_w, np.float32)[:, 0, :]
    cb = np.asarray(L0_ffn_conv_b, np.float32)

    wr = wqkv.reshape(H, 3 * HD, D)
    shared = {
        "wqT": np.ascontiguousarray(wr[:, 0:HD].reshape(D, D).T),
        "wkT": np.ascontiguousarray(wr[:, HD:2 * HD].reshape(D, D).T),
        "wvT": np.ascontiguousarray(wr[:, 2 * HD:3 * HD].reshape(D, D).T),
        "woutT": np.ascontiguousarray(wout.T),
        "w1T": np.ascontiguousarray(w1.T),
        "w2T": np.ascontiguousarray(w2.T),
        "convw": np.ascontiguousarray(cw.reshape(NCC, 128, 3).transpose(1, 0, 2).reshape(128, 96)),
        "convb": np.ascontiguousarray(cb.reshape(NCC, 128).T),
        "costab": costab,
        "sintab": sintab,
        "identb": np.eye(128, dtype=ml_dtypes.bfloat16),
        "identf": np.eye(128, dtype=np.float32),
        "onesf": np.ones((128, 128), np.float32),
        "consts": np.zeros((128, 8), np.float32),
    }
    shared["consts"][:, 0] = 0.75
    shared["consts"][:, 1] = 1e-5
    shared["consts"][:, 2] = 1.0 / D
    sel8 = np.zeros((8, 512), ml_dtypes.bfloat16)
    for v in range(8):
        sel8[v, v * 64:(v + 1) * 64] = 1.0
    shared["sel8"] = sel8
    maps = []
    for b in range(8):
        m = dict(shared)
        m["x_fm"] = np.ascontiguousarray(x[b].T)
        m["x_tm"] = np.ascontiguousarray(x[b])
        maps.append(m)
    return maps


def kernel(**inputs):
    maps = _host_inputs(**inputs)
    nc = build_program()
    res = run_bass_kernel_spmd(nc, maps, core_ids=list(range(8)))
    return np.stack([np.asarray(res.results[b]["y"], dtype=np.float32) for b in range(8)],
                    axis=0)


# version 14
# speedup vs baseline: 1.0963x; 1.0963x over previous
import math
from contextlib import ExitStack

import numpy as np
import ml_dtypes

import concourse.bass as bass
import concourse.mybir as mybir
from concourse.bass_utils import run_bass_kernel_spmd

F32 = mybir.dt.float32
BF16 = mybir.dt.bfloat16
I32 = mybir.dt.int32
OP = mybir.AluOpType
AF = mybir.ActivationFunctionType
AX = mybir.AxisListType.X

S, D, H, HD, C = 2048, 1024, 16, 64, 4096
NST, NDC, NCC, NTC, NPAIR = 16, 8, 32, 16, 8
TWO_PI = 2.0 * math.pi
HSTRIDE = 528
LQ1, LQ2 = 2, 2

_DT = {F32: 4, BF16: 2, I32: 4}
LAYOUT = {}


class Plan:
    ENGINES = ("sync", "gpsimd", "tensor", "vector", "scalar")
    DMA_SEMS = {"LD", "GP", "TP", "OUT", "SP", "L2", "T2", "L3"}

    def __init__(self, nc, es):
        self.nc = nc
        self.q = {e: [] for e in self.ENGINES}
        self.sem = {}
        self.cnt = {}
        for name in ("LD", "GP", "TP", "OUT", "SP", "L2", "T2", "L3",
                     "T", "V", "SC", "G1"):
            self.sem[name] = es.enter_context(nc.semaphore(name))
            self.cnt[name] = 0

    def step(self, eng, fn, waits=(), inc=None):
        h_w = [(self.sem[n], self.cnt[n] if n in self.DMA_SEMS else v)
               for (n, v) in waits if v and v > 0]
        if inc is None:
            def run(e, fn=fn, h_w=h_w):
                for h, v in h_w:
                    e.wait_ge(h, v)
                fn(e)
            self.q[eng].append(run)
            return None
        stepn = 16 if inc in self.DMA_SEMS else 1
        h_i = self.sem[inc]

        def run(e, fn=fn, h_w=h_w, h_i=h_i, stepn=stepn):
            for h, v in h_w:
                e.wait_ge(h, v)
            r = fn(e)
            if r is None:
                r = e.engine_nop()
            r.then_inc(h_i, stepn)
        self.q[eng].append(run)
        self.cnt[inc] += stepn
        return self.cnt[inc]

    def now(self, name):
        return self.cnt[name]

    def barrier(self):
        return [(n, self.cnt[n]) for n in ("T", "V", "SC", "TP", "SP", "LD", "GP", "OUT")
                if self.cnt[n] > 0]


class SubAlloc:

    def __init__(self, tensor, nbytes, esz):
        self.t = tensor
        self.nbytes = nbytes
        self.esz = esz

    def at(self, off, cols):
        assert off % 32 == 0 and off + cols * self.esz <= self.nbytes, (off, cols)
        return self.t[:, off // self.esz: off // self.esz + cols]


def build_program():
    nc = bass.Bass("TRN2", target_bir_lowering=False, debug=False)

    d_xfm = nc.dram_tensor("x_fm", [D, S], F32, kind="ExternalInput")
    d_xtm = nc.dram_tensor("x_tm", [S, D], F32, kind="ExternalInput")
    d_wq = nc.dram_tensor("wqT", [D, D], F32, kind="ExternalInput")
    d_wk = nc.dram_tensor("wkT", [D, D], F32, kind="ExternalInput")
    d_wv = nc.dram_tensor("wvT", [D, D], F32, kind="ExternalInput")
    d_wo = nc.dram_tensor("woutT", [D, D], F32, kind="ExternalInput")
    d_w1 = nc.dram_tensor("w1T", [D, C], F32, kind="ExternalInput")
    d_w2 = nc.dram_tensor("w2T", [C, D], F32, kind="ExternalInput")
    d_cw = nc.dram_tensor("convw", [128, 96], F32, kind="ExternalInput")
    d_cb = nc.dram_tensor("convb", [128, 32], F32, kind="ExternalInput")
    d_cos = nc.dram_tensor("costab", [128, NST * HD], F32, kind="ExternalInput")
    d_sin = nc.dram_tensor("sintab", [128, NST * HD], F32, kind="ExternalInput")
    d_ident = nc.dram_tensor("identb", [128, 128], BF16, kind="ExternalInput")
    d_identf = nc.dram_tensor("identf", [128, 128], F32, kind="ExternalInput")
    d_ones = nc.dram_tensor("onesf", [128, 128], F32, kind="ExternalInput")
    d_sel = nc.dram_tensor("sel8", [8, 512], BF16, kind="ExternalInput")
    d_consts = nc.dram_tensor("consts", [128, 8], F32, kind="ExternalInput")
    d_y = nc.dram_tensor("y", [S, D], F32, kind="ExternalOutput")
    d_x1sp = nc.dram_tensor("x1spill", [S, D], F32, kind="Internal")
    d_hsp = nc.dram_tensor("hspill", [NCC, 128, S], BF16, kind="Internal")
    d_vsp = nc.dram_tensor("vspill", [128, NST * D], BF16, kind="Internal")

    es = ExitStack()
    with es:
        BF_BYTES = 2560 + 40960 + 99328 + 32768
        F_BYTES = 2112 + 24 * 1024
        I_BYTES = 64 + 4096
        abf = SubAlloc(es.enter_context(nc.sbuf_tensor("abf", [128, BF_BYTES // 2], BF16)),
                       BF_BYTES, 2)
        af = SubAlloc(es.enter_context(nc.sbuf_tensor("af", [128, F_BYTES // 4], F32)),
                      F_BYTES, 4)
        ai = SubAlloc(es.enter_context(nc.sbuf_tensor("ai", [128, I_BYTES // 4], I32)),
                      I_BYTES, 4)
        pl = Plan(nc, es)

        sb_ident = abf.at(0, 128)
        sb_sel = abf.at(256, 512)
        lbsb = abf.at(1280, 512)
        rlrow = abf.at(2304, 128)
        BF_A = 2560
        BF_B = BF_A + 40960
        BF_C = BF_B + 99328
        assert BF_C + 32768 == BF_BYTES
        x_fmb = abf.at(BF_A, NDC * S)
        ptm = {0: [abf.at(BF_A, S), abf.at(BF_B + 73728, S),
                   abf.at(BF_B + 81920, S)],
               1: [abf.at(BF_A + 4096, S), abf.at(BF_B + 77824, S),
                   abf.at(BF_B + 86016, S)]}
        PT = [abf.at(BF_A + 8192, NTC * 512), abf.at(BF_A + 24576, NTC * 512)]
        wo_b = abf.at(BF_A, NDC * D)
        xqtm = [abf.at(BF_A + 16384, D), abf.at(BF_A + 18432, D)]
        h2tm = abf.at(BF_A, 4 * C)
        xq2cm = abf.at(BF_A + 32768, NCC * 128)
        qa_w = abf.at(BF_A + 20480, 1024)
        qb_w = abf.at(BF_A + 22528, 1024)
        qd_w = abf.at(BF_A + 24576, 1024)
        q_fm = abf.at(BF_B, NDC * S)
        k_fm = abf.at(BF_B + 32768, NDC * S)
        v_pair = [abf.at(BF_B + 65536, NTC * 128),
                  abf.at(BF_B + 69632, NTC * 128)]
        xqg_fm = abf.at(BF_B, NDC * S)
        tern1 = abf.at(BF_B + 32768, NDC * C)
        tern2 = abf.at(BF_B, NCC * D)
        hsb = abf.at(BF_B + 65536, NCC * HSTRIDE)
        wgrp = abf.at(BF_C, NDC * D)
        rot_buf = [abf.at(BF_C + 16384, D), abf.at(BF_C + 18432, D)]
        attnout = abf.at(BF_C, NDC * S)
        hstage = [abf.at(BF_C, S), abf.at(BF_C + 4096, S)]
        diagb = [abf.at(BF_C, 128), abf.at(BF_C + 256, 128), abf.at(BF_C + 512, 128)]
        qa_c = abf.at(BF_C + 1024, 1024)
        qb_c = abf.at(BF_C + 3072, 1024)
        qd_c = abf.at(BF_C + 5120, 1024)

        sb_consts = af.at(0, 8)
        sb_ones = af.at(32, 128)
        sb_identf = af.at(544, 128)
        sb_cw = af.at(1056, 96)
        sb_cb = af.at(1440, 32)
        sb_vec = af.at(1568, 64)
        nmbuf = af.at(1824, 8)
        lbuf = af.at(1856, 8)
        rlb = af.at(1888, 8)
        g3v = af.at(1920, 4)
        mac1 = af.at(1952, 8)
        mac2 = af.at(1984, 32)
        F_W = 2112
        sb_cos = af.at(F_W, NST * 64)
        sb_sins = af.at(F_W + 4096, NST * 64)
        ropeA = af.at(F_W + 8192, D)
        cos_rep = af.at(F_W + 12288, D)
        sin_rep = af.at(F_W + 16384, D)
        wstage = af.at(F_W, 4096)
        scr = af.at(F_W, D)
        r1b = af.at(F_W + 4096, D)
        x1b = [af.at(F_W + 8192, D), af.at(F_W + 12288, D)]
        xst = af.at(F_W + 16384, D)
        r2b = af.at(F_W + 4096, D)
        outb = [af.at(F_W + 8192, D), af.at(F_W + 12288, D)]
        qscr = af.at(F_W + 20480, D)

        probe_i32 = ai.at(0, 1)
        ti32 = ai.at(64, 64)
        yi32 = ai.at(64, 1024)
        wscr_i32 = ai.at(64, 1024)
        u32 = ai.at(64, 1024)

        LAYOUT.update(BF_A=BF_A, BF_B=BF_B, BF_C=BF_C, F_W=F_W)

        PS_A = es.enter_context(nc.psum_tensor("psA", [128, 2048], F32))
        PS_B = es.enter_context(nc.psum_tensor("psB", [128, 2048], F32))
        ropeB = PS_A[:, 1024:2048]

        def chunk(t, n, i, a, b):
            return t.rearrange("p (n c) -> p n c", n=n)[:, i, a:b]

        def blocked(t, n, a, b):
            return t.rearrange("p (n c) -> p n c", n=n)[:, :, a:b]

        def vs(i):
            return sb_vec[:, i:i + 1]

        V_MW1, V_S1, V_MW2, V_S2, V_OVEC, V_O2 = 0, 1, 2, 3, 4, 5
        V_SCR = 8

        def dma(out, in_, waits=(), sem="LD", eng="sync"):
            return pl.step(eng, lambda e, o=out, i=in_: e.dma_start(out=o, in_=i),
                           waits=waits, inc=sem)

        def dmat(out, in_, waits=(), eng="sync", sem="TP"):
            return pl.step(eng, lambda e, o=out, i=in_: e.dma_start(out=o, in_=i, transpose=True),
                           waits=waits, inc=sem)

        for dst, src in ((sb_consts, d_consts.ap()), (sb_ident, d_ident.ap()),
                         (sb_identf, d_identf.ap()), (sb_ones, d_ones.ap()),
                         (sb_sel[0:8, :], d_sel.ap()), (sb_cw, d_cw.ap()),
                         (sb_cb, d_cb.ap()), (sb_cos, d_cos.ap()),
                         (sb_sins, d_sin.ap())):
            dma(dst, src)
        ld0 = pl.now("LD")

        def cast_load(dst_bf, src_dram, rows, waits=()):
            nch = rows // 128
            return dma(dst_bf.rearrange("p (n c) -> p n c", n=nch),
                       src_dram.ap().rearrange("(n p) c -> p n c", p=128),
                       waits=waits, sem="GP", eng="gpsimd")

        gp_xfm = cast_load(x_fmb, d_xfm, D)

        def qround(e, x_ap, scale_ap, yi, acca, accb, dbf, out, J):
            e.tensor_scalar(out=yi, in0=x_ap, scalar1=scale_ap, scalar2=vs(V_OVEC),
                            op0=OP.mult, op1=OP.add)
            e.drain()
            e.tensor_scalar_add(out=yi, in0=yi, scalar1=-512.0)
            e.drain()
            e.scalar_tensor_tensor(out=dbf, in0=x_ap, scalar=scale_ap, in1=yi,
                                   op0=OP.mult, op1=OP.subtract)
            e.drain()
            ths = [j - 0.5 for j in range(-J + 1, J + 1)]
            e.tensor_scalar(out=acca, in0=dbf, scalar1=ths[0], scalar2=None,
                            op0=OP.is_ge)
            if J > 1:
                e.tensor_scalar(out=accb, in0=dbf, scalar1=ths[1], scalar2=None,
                                op0=OP.is_ge)
            e.drain()
            for i in range(2, 2 * J, 2):
                e.scalar_tensor_tensor(out=acca, in0=dbf, scalar=ths[i], in1=acca,
                                       op0=OP.is_ge, op1=OP.add)
                e.scalar_tensor_tensor(out=accb, in0=dbf, scalar=ths[i + 1], in1=accb,
                                       op0=OP.is_ge, op1=OP.add)
                e.drain()
            if J > 1:
                e.tensor_tensor(out=acca, in0=acca, in1=accb, op=OP.add)
                e.drain()
            else:
                e.scalar_tensor_tensor(out=acca, in0=dbf, scalar=ths[1], in1=acca,
                                       op0=OP.is_ge, op1=OP.add)
                e.drain()
            e.scalar_tensor_tensor(out=out, in0=yi, scalar=float(-J), in1=acca,
                                   op0=OP.add, op1=OP.add)
            e.drain()

        pl.step("vector", lambda e: e.tensor_copy(out=probe_i32, in_=sb_consts[:, 0:1]),
                waits=[("LD", ld0)])
        pl.step("vector", lambda e: e.drain())
        v_tbl = pl.step("vector", lambda e: e.tensor_scalar(out=vs(V_OVEC), in0=probe_i32,
                                                            scalar1=-0.5, scalar2=512.5,
                                                            op0=OP.mult, op1=OP.add),
                        inc="V")

        t_last_g = 0
        last_rd = {0: ("V", 0), 1: ("V", 0)}
        rot_tp = 0
        sp_v = {}
        tp_qk = 0
        sp_v_end = 0
        for gi, wd in enumerate((d_wq, d_wk, d_wv)):
            g = "qkv"[gi]
            gp_w = cast_load(wgrp, wd, D, waits=[("T", t_last_g)])
            for st in range(NST):
                half = st % 2
                psg = (PS_A if half == 0 else PS_B)[:, 0:1024]
                w = [last_rd[half]]
                if st == 0:
                    w += [("GP", gp_w), ("V", v_tbl)]
                t_mm = None
                for K in range(NDC):
                    lhs = chunk(x_fmb, NDC, K, st * 128, st * 128 + 128)
                    for n in range(2):
                        rhs = chunk(wgrp, NDC, K, n * 512, (n + 1) * 512)
                        last = (K == NDC - 1 and n == 1)
                        t_mm = pl.step("tensor", lambda e, o=psg[:, n * 512:(n + 1) * 512],
                                       l=lhs, r=rhs, s0=(K == 0), s1=(K == NDC - 1):
                                       e.matmul(o, l, r, start=s0, stop=s1),
                                       waits=w if (K == 0 and n == 0) else (),
                                       inc="T" if last else None)
                if g in ("q", "k"):
                    def reps(e, st=st):
                        e.drain()
                        for tbl, rep in ((sb_cos, cos_rep), (sb_sins, sin_rep)):
                            e.tensor_copy(out=rep[:, 0:64], in_=tbl[:, st * 64:st * 64 + 64])
                        e.drain()
                        for rep in (cos_rep, sin_rep):
                            for hh in range(1, 16):
                                e.tensor_copy(out=rep[:, hh * 64:hh * 64 + 64],
                                              in_=rep[:, 0:64])
                        e.drain()
                    pl.step("vector", reps, waits=[("T", t_mm)])
                    rb = rot_buf[st % 2]

                    def rope(e, psg=psg, rb=rb):
                        e.tensor_tensor(out=ropeA[:], in0=psg, in1=cos_rep[:], op=OP.mult)
                        sw = psg.rearrange("p (a b) -> p a b", b=2)[:, :, ::-1]
                        e.tensor_tensor(out=ropeB.rearrange("p (a b) -> p a b", b=2),
                                        in0=sw, in1=sin_rep[:].rearrange("p (a b) -> p a b", b=2),
                                        op=OP.mult)
                        e.drain()
                        e.tensor_tensor(out=rb, in0=ropeA[:], in1=ropeB, op=OP.add)
                    v_r = pl.step("vector", rope, waits=[("TP", rot_tp)], inc="V")
                    rot_tp = 1
                    last_rd[half] = ("V", v_r)
                    dstt = q_fm if g == "q" else k_fm
                    dmat(blocked(dstt, NDC, st * 128, st * 128 + 128), rb,
                         waits=[("V", v_r)])
                else:
                    vstg = rot_buf[st % 2]
                    w_v = [("T", t_mm), ("SP", sp_v.get(st - 2, 0))]
                    if st < 2:
                        w_v.append(("TP", tp_qk))
                    sc_v = pl.step(
                        "scalar", lambda e, psg=psg, vstg=vstg: e.activation(
                            out=vstg, in_=psg, func=AF.Copy),
                        waits=w_v, inc="SC")
                    sp_v[st] = dma(d_vsp.ap()[:, st * 1024:(st + 1) * 1024], vstg,
                                   waits=[("SC", sc_v)], sem="SP", eng="gpsimd")
                    last_rd[half] = ("SC", sc_v)
            if g == "k":
                tp_qk = pl.now("TP")
            if g == "v":
                sp_v_end = pl.now("SP")
            t_last_g = pl.now("T")
        tp_p1 = pl.now("TP")
        sc_p1_end = pl.now("SC")

        t_p1_end = pl.now("T")
        wmchunk = [af.at(F_W + 4096 * i, 1024) for i in range(5)]
        macw = af.at(F_W + 20480, 128)
        _p15_srcs = []
        for wd, nch, cwid in ((d_w1, NDC, C), (d_w2, NCC, D)):
            for ch in range(nch):
                for q in range(cwid // 1024):
                    _p15_srcs.append(wd.ap()[ch * 128:(ch + 1) * 128,
                                             q * 1024:(q + 1) * 1024])
        n_p15 = len(_p15_srcs)
        _p15_state = {"nd": 0, "na": 0, "ldn": {}, "sc": {}}

        def p15_pump(flush=False):
            st_ = _p15_state
            while st_["na"] < n_p15:
                k, m = st_["nd"], st_["na"]
                if k < n_p15 and k - m <= 4:
                    if k == 0:
                        w = [("T", t_p1_end), ("TP", tp_p1)]
                    elif k >= 5:
                        w = [("V", st_["sc"][k - 5])]
                    else:
                        w = ()
                    st_["ldn"][k] = dma(wmchunk[k % 5][:], _p15_srcs[k], waits=w,
                                        sem="L2", eng="scalar")
                    st_["nd"] = k + 1
                elif m < k:
                    st_["sc"][m] = pl.step(
                        "vector", lambda e, m=m: e.reduce_sum(
                            out=macw[:, m:m + 1], in_=wmchunk[m % 5][:], axis=AX,
                            apply_absolute_value=True),
                        waits=[("L2", st_["ldn"][m])], inc="V")
                    st_["na"] = m + 1
                if not flush:
                    break

        def emit_means_finale(psum_free_v):
            p15_pump(flush=True)
            pl.step("vector", lambda e: e.drain())
            pl.step("vector", lambda e: e.reduce_sum(out=vs(V_SCR), in_=macw[:, 0:32],
                                                     axis=AX))
            v_mac = pl.step("vector", lambda e: e.reduce_sum(out=vs(V_SCR + 1),
                                                             in_=macw[:, 32:64],
                                                             axis=AX), inc="V")
            for wi in range(2):
                tot = PS_A[0:1, wi:wi + 1]
                t_tot = pl.step("tensor", lambda e, wi=wi, tot=tot: e.matmul(
                    tot, sb_ones[:, 0:1], vs(V_SCR + wi), start=True, stop=True),
                    waits=[("V", max(v_mac, psum_free_v))], inc="T")

                def mseq(e, tot=tot):
                    e.drain()
                    e.tensor_scalar(out=vs(V_SCR + 2)[0:1, :], in0=tot, scalar1=1.0 / (D * C),
                                    scalar2=1e-5, op0=OP.mult, op1=OP.max)
                    e.drain()
                    e.reciprocal(out=vs(V_SCR + 3)[0:1, :], in_=vs(V_SCR + 2)[0:1, :])
                    e.drain()
                v_m = pl.step("vector", mseq, waits=[("T", t_tot)], inc="V")
                pl.step("tensor", lambda e: e.matmul(PS_B[:, 0:1], sb_ones[0:1, 0:128],
                                                     vs(V_SCR + 2)[0:1, :], start=True,
                                                     stop=True),
                        waits=[("V", v_m)])
                t_b = pl.step("tensor", lambda e: e.matmul(
                    PS_B[:, 1:2], sb_ones[0:1, 0:128], vs(V_SCR + 3)[0:1, :],
                    start=True, stop=True), inc="T")
                mw, sc_ = (V_MW1, V_S1) if wi == 0 else (V_MW2, V_S2)
                pl.step("vector", lambda e, mw=mw: e.tensor_copy(out=vs(mw), in_=PS_B[:, 0:1]),
                        waits=[("T", t_b)])
                pl.step("vector", lambda e, sc_=sc_: e.tensor_copy(out=vs(sc_),
                                                                   in_=PS_B[:, 1:2]), inc="V")

        v_p1_end = pl.now("V")
        exp_done = {0: 0, 1: 0}
        ptm_tp = {}
        pt_rd = {0: 0, 1: 0}
        v_evac = 0
        jglob = 0
        tpsem = {0: ("sync", "TP"), 1: ("scalar", "T2")}
        vld = {}
        v_pair_free = {0: 0, 1: 0}

        def v_src(pp):
            return d_vsp.ap().rearrange("r (t c) -> r t c", c=1024)[:, :,
                                                                    pp * 128:(pp + 1) * 128]

        def v_dst(pp):
            return v_pair[pp % 2].rearrange("p (t c) -> p t c", c=128)

        for p in range(NPAIR):
            if p == 0:
                vld[0] = dma(v_dst(0), v_src(0), waits=[("SP", sp_v_end)], sem="L3")
                vld[1] = dma(v_dst(1), v_src(1), sem="L3")
            elif p + 1 < NPAIR:
                vld[p + 1] = dma(v_dst(p + 1), v_src(p + 1),
                                 waits=[("T", v_pair_free[(p + 1) % 2])], sem="L3")
            for scn in range(4):
                exp_last = {}
                tp_last = {0: ("TP", 0), 1: ("TP", 0)}
                for j in range(4):
                    st = scn * 4 + j
                    t_sc = {}
                    v_nm = {}
                    for hb in range(2):
                        PSX = PS_A if hb == 0 else PS_B
                        rows = slice(hb * 64, hb * 64 + 64)
                        lhs = q_fm[rows, p * 2048 + st * 128: p * 2048 + st * 128 + 128]
                        w = [("SC", exp_done[hb]), ("V", v_evac)]
                        if p == 0 and scn == 0 and j == 0:
                            w += [("TP", tp_p1), ("V", v_p1_end), ("SC", sc_p1_end)]
                        p15_pump()
                        for n in range(4):
                            rhs = k_fm[rows, p * 2048 + n * 512: p * 2048 + (n + 1) * 512]
                            t_sc[hb] = pl.step(
                                "tensor", lambda e, o=PSX[:, n * 512:(n + 1) * 512],
                                l=lhs, r=rhs: e.matmul(o, l, r, start=True, stop=True),
                                waits=w if n == 0 else (),
                                inc="T" if n == 3 else None)
                    for hb in range(2):
                        PSX = PS_A if hb == 0 else PS_B
                        col = hb * 4 + j
                        v_nm[hb] = pl.step("vector", lambda e, PSX=PSX, col=col: e.reduce_max(
                            out=nmbuf[:, col:col + 1], in_=PSX[:], axis=AX, negate=True),
                            waits=[("T", t_sc[hb])], inc="V")
                    slot = jglob % 3
                    jglob += 1
                    for hb in range(2):
                        PSX = PS_A if hb == 0 else PS_B
                        col = hb * 4 + j
                        pb = ptm[hb][slot]
                        teng, tsem = tpsem[hb]
                        sc_exp = pl.step("scalar", lambda e, PSX=PSX, pb=pb, col=col:
                                         e.activation(out=pb, in_=PSX[:], func=AF.Exp,
                                                      bias=nmbuf[:, col:col + 1], scale=1.0,
                                                      accum_out=lbuf[:, col:col + 1]),
                                         waits=[("V", v_nm[hb]),
                                                (tsem, ptm_tp.get((hb, slot), 0))],
                                         inc="SC")
                        exp_done[hb] = sc_exp
                        exp_last[(j, hb)] = sc_exp
                        dst = blocked(PT[hb], NTC, j * 128, j * 128 + 128)
                        tpn = dmat(dst, pb, waits=[("SC", sc_exp), ("T", pt_rd[hb])],
                                   eng=teng, sem=tsem)
                        ptm_tp[(hb, slot)] = tpn
                        tp_last[hb] = (tsem, tpn)

                def rl(e):
                    e.reciprocal(out=rlb[:], in_=lbuf[:])
                v_rl = pl.step("vector", rl, waits=[("SC", exp_last[(3, 1)])], inc="V")
                rl_ps = PS_A[0:8, 1024:1152]
                t_rlt = pl.step("tensor", lambda e: e.transpose(rl_ps, rlb[:], sb_identf),
                                waits=[("V", v_rl), ("SC", exp_done[0])], inc="T")
                v_rlrow = pl.step("vector", lambda e: e.tensor_copy(out=rlrow[0:8, :],
                                                                    in_=rl_ps),
                                  waits=[("T", t_rlt)], inc="V")
                for hb in range(2):
                    PSX = PS_A if hb == 0 else PS_B
                    pvs = PSX[hb * 64:hb * 64 + 64, 0:512]
                    lbs = PSX[hb * 64:hb * 64 + 64, 512:1024]
                    w = [tp_last[hb], ("V", v_rlrow), ("SC", exp_done[hb]),
                         ("L3", vld[p])]
                    for tc in range(NTC):
                        lhsv = v_pair[p % 2][:, tc * 128 + hb * 64:
                                             tc * 128 + hb * 64 + 64]
                        rhsp = PT[hb][:, tc * 512:(tc + 1) * 512]
                        pl.step("tensor", lambda e, o=pvs, l=lhsv, r=rhsp, s0=(tc == 0),
                                s1=(tc == NTC - 1): e.matmul(o, l, r, start=s0, stop=s1),
                                waits=w if tc == 0 else ())
                    t_pv = None
                    for jj in range(4):
                        sel = sb_sel[0:8, (hb * 4 + jj) * 64:(hb * 4 + jj) * 64 + 64]
                        t_pv = pl.step("tensor",
                                       lambda e, o=lbs[:, jj * 128:(jj + 1) * 128], sel=sel:
                                       e.matmul(o, sel, rlrow[0:8, :], start=True, stop=True),
                                       inc="T" if jj == 3 else None)
                    pt_rd[hb] = t_pv
                    dst = attnout[hb * 64:hb * 64 + 64,
                                  p * 2048 + scn * 512: p * 2048 + (scn + 1) * 512]
                    lrows = lbsb[hb * 64:hb * 64 + 64, :]

                    def pv_evac(e, dst=dst, pvs=pvs, lbs=lbs, lrows=lrows):
                        e.tensor_copy(out=lrows, in_=lbs)
                        e.drain()
                        e.tensor_tensor(out=dst, in0=pvs, in1=lrows, op=OP.mult)
                    v_evac = pl.step("vector", pv_evac, waits=[("T", t_pv)], inc="V")
                exp_done = {0: 0, 1: 0}
            v_pair_free[p % 2] = pl.now("T")
        emit_means_finale(pl.now("V"))
        v_attn = pl.now("V")

        gp_wo = cast_load(wo_b, d_wo, D, waits=pl.barrier())
        sp_x1 = {}
        v_lnst = {}
        v_q1d = {}
        xq_tp = {}
        x_free_v = 0
        for st in range(NST):
            psw = PS_A[:, 0:1024] if st % 2 == 0 else PS_A[:, 1024:2048]
            x_ld = dma(xst, d_xtm.ap()[st * 128:(st + 1) * 128, :],
                       waits=[("V", x_free_v if st else v_attn)])
            w = [("V", v_lnst.get(st - 2, 0))]
            if st == 0:
                w += [("GP", gp_wo), ("V", v_attn)]
            t_wo = None
            for K in range(NDC):
                lhs = attnout[:, K * 2048 + st * 128: K * 2048 + st * 128 + 128]
                for n in range(2):
                    last = (K == NDC - 1 and n == 1)
                    rhs = chunk(wo_b, NDC, K, n * 512, (n + 1) * 512)
                    t_wo = pl.step("tensor", lambda e, o=psw[:, n * 512:(n + 1) * 512], l=lhs,
                                   r=rhs, s0=(K == 0), s1=(K == NDC - 1):
                                   e.matmul(o, l, r, start=s0, stop=s1),
                                   waits=w if (K == 0 and n == 0) else (),
                                   inc="T" if last else None)

            def lnst(e, psw=psw):
                e.tensor_tensor(out=r1b[:], in0=psw, in1=xst, op=OP.add)
                e.drain()
                e.reduce_sum(out=vs(V_SCR), in_=r1b[:], axis=AX)
                e.drain()
                e.tensor_scalar_mul(out=vs(V_SCR + 1), in0=vs(V_SCR), scalar1=-1.0 / D)
                e.drain()
                e.scalar_tensor_tensor(out=scr[:], in0=r1b[:], scalar=vs(V_SCR + 1),
                                       in1=r1b[:], op0=OP.add, op1=OP.mult,
                                       accum_out=vs(V_SCR + 2))
            v_st = pl.step("vector", lnst, waits=[("T", t_wo), ("LD", x_ld)], inc="V")
            v_lnst[st] = v_st
            x_free_v = v_st
            sc_sq = pl.step("scalar", lambda e: e.activation(
                out=vs(V_SCR + 3), in_=vs(V_SCR + 2), func=AF.Sqrt,
                bias=sb_consts[:, 1:2], scale=sb_consts[:, 2:3]),
                waits=[("V", v_st)], inc="SC")

            def rstd(e):
                e.reciprocal(out=vs(V_SCR + 4), in_=vs(V_SCR + 3))
                e.drain()
                e.tensor_mul(out=vs(V_SCR + 5), in0=vs(V_SCR + 1), in1=vs(V_SCR + 4))
            v_rs = pl.step("vector", rstd, waits=[("SC", sc_sq)], inc="V")
            x1t = x1b[st % 2]

            def ln1apply(e, x1t=x1t):
                e.drain()
                e.tensor_scalar(out=x1t[:], in0=r1b[:], scalar1=vs(V_SCR + 4),
                                scalar2=vs(V_SCR + 5), op0=OP.mult, op1=OP.add)
            v_x1 = pl.step("vector", ln1apply,
                           waits=[("SP", sp_x1.get(st - 2, 0)),
                                  ("V", v_q1d.get(st - 2, 0))], inc="V")
            sp_x1[st] = dma(d_x1sp.ap()[st * 128:(st + 1) * 128, :], x1t[:],
                            waits=[("V", v_x1)], sem="SP", eng="gpsimd")
            xq = xqtm[st % 2]

            def q1(e, x1t=x1t, xq=xq):
                e.drain()
                e.reduce_max(out=vs(V_SCR), in_=x1t[:], axis=AX, apply_absolute_value=True)
                e.drain()
                e.tensor_scalar_max(out=vs(V_SCR + 1), in0=vs(V_SCR), scalar1=1e-5)
                e.drain()
                e.tensor_scalar_mul(out=vs(V_SCR + 2), in0=vs(V_SCR + 1), scalar1=1.0 / 127.0)
                e.tensor_scalar(out=vs(V_SCR + 4), in0=vs(V_SCR + 1), scalar1=vs(V_MW1),
                                scalar2=1.0 / 127.0, op0=OP.mult, op1=OP.mult)
                e.drain()
                e.reciprocal(out=vs(V_SCR + 3), in_=vs(V_SCR + 2))
                e.drain()
                qround(e, x1t[:], vs(V_SCR + 3), yi32[:], qa_w[:], qb_w[:],
                       qd_w[:], qscr[:], LQ1)
                e.tensor_scalar_mul(out=xq, in0=qscr[:], scalar1=vs(V_SCR + 4))
            v_q1 = pl.step("vector", q1, waits=[("TP", xq_tp.get(st - 2, 0))], inc="V")
            v_q1d[st] = v_q1
            xq_tp[st] = dmat(blocked(xqg_fm, NDC, st * 128, st * 128 + 128), xq,
                             waits=[("V", v_q1)])
        tp_xq = pl.now("TP")

        def ternarize(wd, nch, cw, tern, scslot, waits0):
            v_done = 0
            for ch in range(nch):
                w = list(waits0) if ch == 0 else []
                w.append(("V", v_done))
                ldn = dma(wstage[:, 0:cw], wd.ap()[ch * 128:(ch + 1) * 128, :], waits=w)
                dstt = chunk(tern, nch, ch, 0, cw)

                def tn(e, cw=cw, dstt=dstt, scslot=scslot):
                    e.drain()
                    for q0 in range(0, cw, 1024):
                        dch = dstt.rearrange("p (n c) -> p n c", n=cw // 1024)[:, q0 // 1024, :]
                        qround(e, wstage[:, q0:q0 + 1024], vs(scslot), wscr_i32[:],
                               qa_w[:], qb_w[:], qd_w[:], xst, 1)
                        e.tensor_scalar(out=dch, in0=xst, scalar1=1.0,
                                        scalar2=-1.0, op0=OP.min, op1=OP.max)
                        e.drain()
                v_done = pl.step("vector", tn, waits=[("LD", ldn)], inc="V")
            return v_done

        v_tern1 = ternarize(d_w1, NDC, C, tern1, V_S1, pl.barrier())

        v_hev = {}
        sp_h = {}
        for cc in range(NCC):
            psh = PS_A if cc % 2 == 0 else PS_B
            w = [("SC", v_hev.get(cc - 2, 0))]
            if cc == 0:
                w += [("V", v_tern1), ("TP", tp_xq)]
            t_h = None
            for K in range(NDC):
                lhs = chunk(tern1, NDC, K, cc * 128, cc * 128 + 128)
                for n in range(4):
                    last = (K == NDC - 1 and n == 3)
                    rhs = chunk(xqg_fm, NDC, K, n * 512, (n + 1) * 512)
                    t_h = pl.step("tensor", lambda e, o=psh[:, n * 512:(n + 1) * 512], l=lhs,
                                  r=rhs, s0=(K == 0), s1=(K == NDC - 1):
                                  e.matmul(o, l, r, start=s0, stop=s1),
                                  waits=w if (K == 0 and n == 0) else (),
                                  inc="T" if last else None)
            hb = hstage[cc % 2]
            v_hev[cc] = pl.step("scalar", lambda e, psh=psh, hb=hb: e.activation(
                out=hb, in_=psh[:], func=AF.Copy),
                waits=[("T", t_h), ("SP", sp_h.get(cc - 2, 0))], inc="SC")
            sp_h[cc] = dma(d_hsp.ap()[cc, :, :], hb, waits=[("SC", v_hev[cc])],
                           sem="SP", eng="gpsimd")

        v_tern2 = ternarize(d_w2, NCC, D, tern2, V_S2, pl.barrier())

        out_dma = {}
        t_conv_last = 0
        sc_silu_last = 0
        v_q2_last = 0
        t_w2_last = 0
        v_ev2 = {}
        for scn in range(4):
            lo, hi = scn * 512 - 1, scn * 512 + 513
            slo, shi = max(lo, 0), min(hi, S)
            doff = slo - lo
            src = d_hsp.ap()[:, :, slo:shi].transpose([1, 0, 2])
            dst = hsb.rearrange("p (n c) -> p n c", c=HSTRIDE)[:, :, doff:doff + (shi - slo)]
            w = [("T", t_conv_last), ("TP", pl.now("TP"))]
            if scn == 0:
                w = pl.barrier()
            ld_h = dma(dst, src, waits=w)

            def edges(e, scn=scn):
                hv = hsb.rearrange("p (n c) -> p n c", c=HSTRIDE)
                if scn == 0:
                    e.memset(hv[:, :, 0:1], 0.0)
                if scn == 3:
                    e.memset(hv[:, :, 513:514], 0.0)
                e.memset(hv[:, :, 514:HSTRIDE], 0.0)
            v_hld = pl.step("vector", edges, waits=[("LD", ld_h)], inc="V")
            sc_silu = {}
            for cc in range(NCC):
                psc = PS_A[:, (cc % 4) * 512:((cc % 4) + 1) * 512]

                def dg(e, cc=cc):
                    e.tensor_scalar_mul(out=diagb[0], in0=sb_ident,
                                        scalar1=sb_cw[:, cc * 3:cc * 3 + 1])
                    e.tensor_scalar_mul(out=diagb[1], in0=sb_ident,
                                        scalar1=sb_cw[:, cc * 3 + 1:cc * 3 + 2])
                    e.tensor_scalar_mul(out=diagb[2], in0=sb_ident,
                                        scalar1=sb_cw[:, cc * 3 + 2:cc * 3 + 3])
                v_d = pl.step("vector", dg, waits=[("V", v_hld), ("T", t_conv_last)], inc="V")
                w = [("V", v_d), ("SC", sc_silu.get(cc - 4, sc_silu_last))]
                t_cv = None
                for k in range(3):
                    rhs = hsb.rearrange("p (n c) -> p n c", c=HSTRIDE)[:, cc, k:k + 512]
                    t_cv = pl.step("tensor", lambda e, o=psc, dk=diagb[k], r=rhs, s0=(k == 0),
                                   s1=(k == 2): e.matmul(o, dk, r, start=s0, stop=s1),
                                   waits=w if k == 0 else (),
                                   inc="T" if k == 2 else None)
                t_conv_last = t_cv
                silu_dst = hsb[:, cc * HSTRIDE:cc * HSTRIDE + 512]
                sc_silu[cc] = pl.step("scalar", lambda e, psc=psc, sd=silu_dst, cc=cc:
                                      e.activation(out=sd, in_=psc, func=AF.Silu,
                                                   bias=sb_cb[:, cc:cc + 1], scale=1.0),
                                      waits=[("T", t_cv)], inc="SC")
            sc_silu_last = sc_silu[NCC - 1]
            tp_h2 = 0
            for cc in range(NCC):
                silu_src = hsb[:, cc * HSTRIDE:cc * HSTRIDE + 512]
                dst = blocked(h2tm, 4, cc * 128, (cc + 1) * 128)
                tp_h2 = dmat(dst, silu_src,
                             waits=[("SC", sc_silu[cc]), ("V", v_q2_last)])
            tp_bk = {}
            for j in range(4):
                st = scn * 4 + j
                h2 = h2tm[:, j * C:(j + 1) * C]

                def q2(e, h2=h2, j=j):
                    e.drain()
                    e.reduce_max(out=vs(V_SCR), in_=h2, axis=AX, apply_absolute_value=True)
                    e.drain()
                    e.tensor_scalar_max(out=vs(V_SCR + 1), in0=vs(V_SCR), scalar1=1e-5)
                    e.drain()
                    e.tensor_scalar_mul(out=vs(V_SCR + 2), in0=vs(V_SCR + 1), scalar1=1.0 / 127.0)
                    e.tensor_scalar(out=g3v[:, j:j + 1], in0=vs(V_SCR + 1), scalar1=vs(V_MW2),
                                    scalar2=1.0 / 127.0, op0=OP.mult, op1=OP.mult)
                    e.drain()
                    e.reciprocal(out=vs(V_SCR + 3), in_=vs(V_SCR + 2))
                    e.drain()
                    for q0 in range(0, C, 1024):
                        qround(e, h2[:, q0:q0 + 1024], vs(V_SCR + 3), u32[:],
                               qa_c[:], qb_c[:], qd_c[:], h2[:, q0:q0 + 1024], LQ2)
                v_q2 = pl.step("vector", q2, waits=[("TP", tp_h2)], inc="V")
                v_q2_last = v_q2
                tp_bk[j] = dmat(blocked(xq2cm, NCC, 0, 128), h2,
                                waits=[("V", v_q2), ("T", t_w2_last)])
                psw = PS_B[:, 0:1024] if j % 2 == 0 else PS_B[:, 1024:2048]
                w = [("TP", tp_bk[j]), ("V", v_ev2.get((scn, j - 2), 0))]
                if scn > 0 and j < 2:
                    w.append(("V", v_ev2.get((scn - 1, j + 2), 0)))
                t_w2 = None
                for cc in range(NCC):
                    lhs = chunk(xq2cm, NCC, cc, 0, 128)
                    for n in range(2):
                        last = (cc == NCC - 1 and n == 1)
                        rhs = chunk(tern2, NCC, cc, n * 512, (n + 1) * 512)
                        t_w2 = pl.step("tensor", lambda e, o=psw[:, n * 512:(n + 1) * 512],
                                       l=lhs, r=rhs, s0=(cc == 0), s1=(cc == NCC - 1):
                                       e.matmul(o, l, r, start=s0, stop=s1),
                                       waits=w if (cc == 0 and n == 0) else (),
                                       inc="T" if last else None)
                t_w2_last = t_w2
                ld_x1 = dma(xst, d_x1sp.ap()[st * 128:(st + 1) * 128, :],
                            waits=[("V", v_ev2.get((scn, j - 1),
                                                   v_ev2.get((scn - 1, 3), 0)) if (j or scn)
                                    else v_q2)])

                def ev2(e, psw=psw, j=j):
                    e.drain()
                    e.scalar_tensor_tensor(out=r2b[:], in0=psw, scalar=g3v[:, j:j + 1],
                                           in1=xst, op0=OP.mult, op1=OP.add)
                    e.drain()
                    e.reduce_sum(out=vs(V_SCR), in_=r2b[:], axis=AX)
                    e.drain()
                    e.tensor_scalar_mul(out=vs(V_SCR + 1), in0=vs(V_SCR), scalar1=-1.0 / D)
                    e.drain()
                    e.scalar_tensor_tensor(out=scr[:], in0=r2b[:], scalar=vs(V_SCR + 1),
                                           in1=r2b[:], op0=OP.add, op1=OP.mult,
                                           accum_out=vs(V_SCR + 2))
                v_s2 = pl.step("vector", ev2, waits=[("T", t_w2), ("LD", ld_x1)], inc="V")
                v_ev2[(scn, j)] = v_s2
                sc_sq2 = pl.step("scalar", lambda e: e.activation(
                    out=vs(V_SCR + 3), in_=vs(V_SCR + 2), func=AF.Sqrt,
                    bias=sb_consts[:, 1:2], scale=sb_consts[:, 2:3]),
                    waits=[("V", v_s2)], inc="SC")

                def rstd2(e):
                    e.reciprocal(out=vs(V_SCR + 4), in_=vs(V_SCR + 3))
                    e.drain()
                    e.tensor_mul(out=vs(V_SCR + 5), in0=vs(V_SCR + 1), in1=vs(V_SCR + 4))
                v_r2 = pl.step("vector", rstd2, waits=[("SC", sc_sq2)], inc="V")
                ob = outb[j % 2]

                def ln2apply(e, ob=ob):
                    e.drain()
                    e.tensor_scalar(out=ob[:], in0=r2b[:], scalar1=vs(V_SCR + 4),
                                    scalar2=vs(V_SCR + 5), op0=OP.mult, op1=OP.add)
                sc_out = pl.step("vector", ln2apply,
                                 waits=[("OUT", out_dma.get(st - 2, 0))], inc="V")
                out_dma[st] = dma(d_y.ap()[st * 128:(st + 1) * 128, :], ob[:],
                                  waits=[("V", sc_out)], sem="OUT")

        pl.step("sync", lambda e: e.nop(), waits=[("OUT", pl.now("OUT"))])
        pl.step("gpsimd", lambda e: e.nop(), waits=[("SP", pl.now("SP"))])

        with nc.Block() as blk:
            @blk.sync
            def _(e):
                for fn in pl.q["sync"]:
                    fn(e)

            @blk.gpsimd
            def _(e):
                for fn in pl.q["gpsimd"]:
                    fn(e)

            @blk.tensor
            def _(e):
                for fn in pl.q["tensor"]:
                    fn(e)

            @blk.vector
            def _(e):
                for fn in pl.q["vector"]:
                    fn(e)

            @blk.scalar
            def _(e):
                for fn in pl.q["scalar"]:
                    fn(e)

    return nc


def _rope_tables():
    import jax
    import jax.numpy as jnp
    with jax.default_device(jax.devices("cpu")[0]):
        pos = jnp.arange(S, dtype=jnp.float32)[:, None]
        ids = jnp.arange(HD // 2, dtype=jnp.float32)
        theta = jnp.power(jnp.float32(10000.0), 2.0 * ids / HD)
        ang = pos / theta
        sn = np.asarray(jnp.sin(ang), np.float32)
        cs = np.asarray(jnp.cos(ang), np.float32)
    cosr = np.repeat(cs, 2, axis=1)
    sinr = np.repeat(sn, 2, axis=1) * np.tile(
        np.array([-1.0, 1.0], np.float32), HD // 2)
    tab = lambda a: np.ascontiguousarray(
        a.reshape(NST, 128, HD).transpose(1, 0, 2).reshape(128, NST * HD))
    return tab(cosr), tab(sinr)


def _host_inputs(L0_attn_x, L0_attn_w_qkv, L0_attn_w_out, L0_ffn_w1,
                 L0_ffn_conv_w, L0_ffn_conv_b, L0_ffn_w2):
    costab, sintab = _rope_tables()
    x = np.ascontiguousarray(np.asarray(L0_attn_x, np.float32))
    wqkv = np.asarray(L0_attn_w_qkv, np.float32)
    wout = np.asarray(L0_attn_w_out, np.float32)
    w1 = np.asarray(L0_ffn_w1, np.float32)
    w2 = np.asarray(L0_ffn_w2, np.float32)
    cw = np.asarray(L0_ffn_conv_w, np.float32)[:, 0, :]
    cb = np.asarray(L0_ffn_conv_b, np.float32)

    wr = wqkv.reshape(H, 3 * HD, D)
    shared = {
        "wqT": np.ascontiguousarray(wr[:, 0:HD].reshape(D, D).T),
        "wkT": np.ascontiguousarray(wr[:, HD:2 * HD].reshape(D, D).T),
        "wvT": np.ascontiguousarray(wr[:, 2 * HD:3 * HD].reshape(D, D).T),
        "woutT": np.ascontiguousarray(wout.T),
        "w1T": np.ascontiguousarray(w1.T),
        "w2T": np.ascontiguousarray(w2.T),
        "convw": np.ascontiguousarray(cw.reshape(NCC, 128, 3).transpose(1, 0, 2).reshape(128, 96)),
        "convb": np.ascontiguousarray(cb.reshape(NCC, 128).T),
        "costab": costab,
        "sintab": sintab,
        "identb": np.eye(128, dtype=ml_dtypes.bfloat16),
        "identf": np.eye(128, dtype=np.float32),
        "onesf": np.ones((128, 128), np.float32),
        "consts": np.zeros((128, 8), np.float32),
    }
    shared["consts"][:, 0] = 0.75
    shared["consts"][:, 1] = 1e-5
    shared["consts"][:, 2] = 1.0 / D
    sel8 = np.zeros((8, 512), ml_dtypes.bfloat16)
    for v in range(8):
        sel8[v, v * 64:(v + 1) * 64] = 1.0
    shared["sel8"] = sel8
    maps = []
    for b in range(8):
        m = dict(shared)
        m["x_fm"] = np.ascontiguousarray(x[b].T)
        m["x_tm"] = np.ascontiguousarray(x[b])
        maps.append(m)
    return maps


def kernel(**inputs):
    maps = _host_inputs(**inputs)
    nc = build_program()
    res = run_bass_kernel_spmd(nc, maps, core_ids=list(range(8)))
    return np.stack([np.asarray(res.results[b]["y"], dtype=np.float32) for b in range(8)],
                    axis=0)


# version 17
# speedup vs baseline: 1.1899x; 1.0854x over previous
import math
from contextlib import ExitStack

import numpy as np
import ml_dtypes

import concourse.bass as bass
import concourse.mybir as mybir
from concourse.bass_utils import run_bass_kernel_spmd

F32 = mybir.dt.float32
BF16 = mybir.dt.bfloat16
I32 = mybir.dt.int32
OP = mybir.AluOpType
AF = mybir.ActivationFunctionType
AX = mybir.AxisListType.X

S, D, H, HD, C = 2048, 1024, 16, 64, 4096
NST, NDC, NCC, NTC, NPAIR = 16, 8, 32, 16, 8
TWO_PI = 2.0 * math.pi
HSTRIDE = 528
LQ1, LQ2 = 2, 2

_DT = {F32: 4, BF16: 2, I32: 4}
LAYOUT = {}


class Plan:
    ENGINES = ("sync", "gpsimd", "tensor", "vector", "scalar")
    DMA_SEMS = {"LD", "GP", "TP", "OUT", "SP", "L2", "T2", "L3"}

    def __init__(self, nc, es):
        self.nc = nc
        self.q = {e: [] for e in self.ENGINES}
        self.sem = {}
        self.cnt = {}
        for name in ("LD", "GP", "TP", "OUT", "SP", "L2", "T2", "L3",
                     "T", "V", "SC", "G1"):
            self.sem[name] = es.enter_context(nc.semaphore(name))
            self.cnt[name] = 0

    def step(self, eng, fn, waits=(), inc=None):
        h_w = [(self.sem[n], self.cnt[n] if n in self.DMA_SEMS else v)
               for (n, v) in waits if v and v > 0]
        if inc is None:
            def run(e, fn=fn, h_w=h_w):
                for h, v in h_w:
                    e.wait_ge(h, v)
                fn(e)
            self.q[eng].append(run)
            return None
        stepn = 16 if inc in self.DMA_SEMS else 1
        h_i = self.sem[inc]

        def run(e, fn=fn, h_w=h_w, h_i=h_i, stepn=stepn):
            for h, v in h_w:
                e.wait_ge(h, v)
            r = fn(e)
            if r is None:
                r = e.engine_nop()
            r.then_inc(h_i, stepn)
        self.q[eng].append(run)
        self.cnt[inc] += stepn
        return self.cnt[inc]

    def now(self, name):
        return self.cnt[name]

    def barrier(self):
        return [(n, self.cnt[n]) for n in ("T", "V", "SC", "TP", "SP", "LD", "GP", "OUT")
                if self.cnt[n] > 0]


class SubAlloc:

    def __init__(self, tensor, nbytes, esz):
        self.t = tensor
        self.nbytes = nbytes
        self.esz = esz

    def at(self, off, cols):
        assert off % 32 == 0 and off + cols * self.esz <= self.nbytes, (off, cols)
        return self.t[:, off // self.esz: off // self.esz + cols]


def build_program():
    nc = bass.Bass("TRN2", target_bir_lowering=False, debug=False)

    d_xfm = nc.dram_tensor("x_fm", [D, S], F32, kind="ExternalInput")
    d_xtm = nc.dram_tensor("x_tm", [S, D], F32, kind="ExternalInput")
    d_wq = nc.dram_tensor("wqT", [D, D], F32, kind="ExternalInput")
    d_wk = nc.dram_tensor("wkT", [D, D], F32, kind="ExternalInput")
    d_wv = nc.dram_tensor("wvT", [D, D], F32, kind="ExternalInput")
    d_wo = nc.dram_tensor("woutT", [D, D], F32, kind="ExternalInput")
    d_w1 = nc.dram_tensor("w1T", [D, C], F32, kind="ExternalInput")
    d_w2 = nc.dram_tensor("w2T", [C, D], F32, kind="ExternalInput")
    d_cw = nc.dram_tensor("convw", [128, 96], F32, kind="ExternalInput")
    d_cb = nc.dram_tensor("convb", [128, 32], F32, kind="ExternalInput")
    d_cos = nc.dram_tensor("costab", [128, NST * HD], F32, kind="ExternalInput")
    d_sin = nc.dram_tensor("sintab", [128, NST * HD], F32, kind="ExternalInput")
    d_ident = nc.dram_tensor("identb", [128, 128], BF16, kind="ExternalInput")
    d_identf = nc.dram_tensor("identf", [128, 128], F32, kind="ExternalInput")
    d_ones = nc.dram_tensor("onesf", [128, 128], F32, kind="ExternalInput")
    d_sel = nc.dram_tensor("sel8", [8, 512], BF16, kind="ExternalInput")
    d_consts = nc.dram_tensor("consts", [128, 8], F32, kind="ExternalInput")
    d_y = nc.dram_tensor("y", [S, D], F32, kind="ExternalOutput")
    d_x1sp = nc.dram_tensor("x1spill", [S, D], F32, kind="Internal")
    d_hsp = nc.dram_tensor("hspill", [NCC, 128, S], BF16, kind="Internal")
    d_vsp = nc.dram_tensor("vspill", [128, NST * D], BF16, kind="Internal")

    es = ExitStack()
    with es:
        BF_BYTES = 2560 + 40960 + 99328 + 32768
        F_BYTES = 2112 + 24 * 1024
        I_BYTES = 64 + 8192
        abf = SubAlloc(es.enter_context(nc.sbuf_tensor("abf", [128, BF_BYTES // 2], BF16)),
                       BF_BYTES, 2)
        af = SubAlloc(es.enter_context(nc.sbuf_tensor("af", [128, F_BYTES // 4], F32)),
                      F_BYTES, 4)
        ai = SubAlloc(es.enter_context(nc.sbuf_tensor("ai", [128, I_BYTES // 4], I32)),
                      I_BYTES, 4)
        pl = Plan(nc, es)

        sb_ident = abf.at(0, 128)
        sb_sel = abf.at(256, 512)
        lbsb = abf.at(1280, 512)
        rlrow = abf.at(2304, 128)
        BF_A = 2560
        BF_B = BF_A + 40960
        BF_C = BF_B + 99328
        assert BF_C + 32768 == BF_BYTES
        x_fmb = abf.at(BF_A, NDC * S)
        ptm = {0: [abf.at(BF_A, S), abf.at(BF_B + 73728, S),
                   abf.at(BF_B + 81920, S)],
               1: [abf.at(BF_A + 4096, S), abf.at(BF_B + 77824, S),
                   abf.at(BF_B + 86016, S)]}
        PT = [abf.at(BF_A + 8192, NTC * 512), abf.at(BF_A + 24576, NTC * 512)]
        wo_b = abf.at(BF_A, NDC * D)
        xqtm = [abf.at(BF_A + 16384, D), abf.at(BF_A + 18432, D)]
        h2tm = abf.at(BF_A, 4 * C)
        xq2cm = abf.at(BF_A + 32768, NCC * 128)
        qa_w = abf.at(BF_A + 20480, 1024)
        qb_w = abf.at(BF_A + 22528, 1024)
        qd_w = abf.at(BF_A + 24576, 1024)
        qa2_w = abf.at(BF_A + 26624, 1024)
        qb2_w = abf.at(BF_A + 28672, 1024)
        qd2_w = abf.at(BF_A + 30720, 1024)
        q_fm = abf.at(BF_B, NDC * S)
        k_fm = abf.at(BF_B + 32768, NDC * S)
        v_pair = [abf.at(BF_B + 65536, NTC * 128),
                  abf.at(BF_B + 69632, NTC * 128)]
        xqg_fm = abf.at(BF_B, NDC * S)
        tern1 = abf.at(BF_B + 32768, NDC * C)
        tern2 = abf.at(BF_B, NCC * D)
        hsb = abf.at(BF_B + 65536, NCC * HSTRIDE)
        wgrp = abf.at(BF_C, NDC * D)
        rot_buf = [abf.at(BF_C + 16384, D), abf.at(BF_C + 18432, D)]
        attnout = abf.at(BF_C, NDC * S)
        hstage = [abf.at(BF_C, S), abf.at(BF_C + 4096, S)]
        diagb = [abf.at(BF_C, 128), abf.at(BF_C + 256, 128), abf.at(BF_C + 512, 128)]
        qa_c = abf.at(BF_C + 1024, 1024)
        qb_c = abf.at(BF_C + 3072, 1024)
        qd_c = abf.at(BF_C + 5120, 1024)
        qa2_c = abf.at(BF_C + 7168, 1024)
        qb2_c = abf.at(BF_C + 9216, 1024)
        qd2_c = abf.at(BF_C + 11264, 1024)

        sb_consts = af.at(0, 8)
        sb_ones = af.at(32, 128)
        sb_identf = af.at(544, 128)
        sb_cw = af.at(1056, 96)
        sb_cb = af.at(1440, 32)
        sb_vec = af.at(1568, 64)
        nmbuf = af.at(1824, 8)
        lbuf = af.at(1856, 8)
        rlb = af.at(1888, 8)
        g3v = af.at(1920, 4)
        mac1 = af.at(1952, 8)
        mac2 = af.at(1984, 32)
        F_W = 2112
        sb_cos = af.at(F_W, NST * 64)
        sb_sins = af.at(F_W + 4096, NST * 64)
        ropeA = af.at(F_W + 8192, D)
        cos_rep = af.at(F_W + 12288, D)
        sin_rep = af.at(F_W + 16384, D)
        wstage = af.at(F_W, 4096)
        scr = af.at(F_W, D)
        r1b = af.at(F_W + 4096, D)
        x1b = [af.at(F_W + 8192, D), af.at(F_W + 12288, D)]
        xst = af.at(F_W + 16384, D)
        r2b = af.at(F_W + 4096, D)
        outb = [af.at(F_W + 8192, D), af.at(F_W + 12288, D)]
        qscr = af.at(F_W + 20480, D)

        probe_i32 = ai.at(0, 1)
        ti32 = ai.at(64, 64)
        yi32 = ai.at(64, 1024)
        wscr_i32 = ai.at(64, 1024)
        u32 = ai.at(64, 1024)
        u32b = ai.at(4160, 1024)

        LAYOUT.update(BF_A=BF_A, BF_B=BF_B, BF_C=BF_C, F_W=F_W)

        PS_A = es.enter_context(nc.psum_tensor("psA", [128, 2048], F32))
        PS_B = es.enter_context(nc.psum_tensor("psB", [128, 2048], F32))
        ropeB = PS_A[:, 1024:2048]

        def chunk(t, n, i, a, b):
            return t.rearrange("p (n c) -> p n c", n=n)[:, i, a:b]

        def blocked(t, n, a, b):
            return t.rearrange("p (n c) -> p n c", n=n)[:, :, a:b]

        def vs(i):
            return sb_vec[:, i:i + 1]

        V_MW1, V_S1, V_MW2, V_S2, V_OVEC, V_O2 = 0, 1, 2, 3, 4, 5
        V_SCR = 8

        def dma(out, in_, waits=(), sem="LD", eng="sync"):
            return pl.step(eng, lambda e, o=out, i=in_: e.dma_start(out=o, in_=i),
                           waits=waits, inc=sem)

        def dmat(out, in_, waits=(), eng="sync", sem="TP"):
            return pl.step(eng, lambda e, o=out, i=in_: e.dma_start(out=o, in_=i, transpose=True),
                           waits=waits, inc=sem)

        for dst, src in ((sb_consts, d_consts.ap()), (sb_ident, d_ident.ap()),
                         (sb_identf, d_identf.ap()), (sb_ones, d_ones.ap()),
                         (sb_sel[0:8, :], d_sel.ap()), (sb_cw, d_cw.ap()),
                         (sb_cb, d_cb.ap()), (sb_cos, d_cos.ap()),
                         (sb_sins, d_sin.ap())):
            dma(dst, src)
        ld0 = pl.now("LD")

        def cast_load(dst_bf, src_dram, rows, waits=()):
            nch = rows // 128
            return dma(dst_bf.rearrange("p (n c) -> p n c", n=nch),
                       src_dram.ap().rearrange("(n p) c -> p n c", p=128),
                       waits=waits, sem="GP", eng="gpsimd")

        gp_xfm = cast_load(x_fmb, d_xfm, D)

        def qround_multi(e, items, scale_ap, J):
            ths = [j - 0.5 for j in range(-J + 1, J + 1)]
            for x_ap, yi, acca, accb, dbf, out in items:
                e.tensor_scalar(out=yi, in0=x_ap, scalar1=scale_ap,
                                scalar2=vs(V_OVEC), op0=OP.mult, op1=OP.add)
            e.drain()
            for x_ap, yi, acca, accb, dbf, out in items:
                e.tensor_scalar_add(out=yi, in0=yi, scalar1=-512.0)
            e.drain()
            for x_ap, yi, acca, accb, dbf, out in items:
                e.scalar_tensor_tensor(out=dbf, in0=x_ap, scalar=scale_ap, in1=yi,
                                       op0=OP.mult, op1=OP.subtract)
            e.drain()
            for x_ap, yi, acca, accb, dbf, out in items:
                e.tensor_scalar(out=acca, in0=dbf, scalar1=ths[0], scalar2=None,
                                op0=OP.is_ge)
                e.tensor_scalar(out=accb, in0=dbf, scalar1=ths[1], scalar2=None,
                                op0=OP.is_ge)
            e.drain()
            for i in range(2, 2 * J, 2):
                for x_ap, yi, acca, accb, dbf, out in items:
                    e.scalar_tensor_tensor(out=acca, in0=dbf, scalar=ths[i], in1=acca,
                                           op0=OP.is_ge, op1=OP.add)
                    e.scalar_tensor_tensor(out=accb, in0=dbf, scalar=ths[i + 1],
                                           in1=accb, op0=OP.is_ge, op1=OP.add)
                e.drain()
            for x_ap, yi, acca, accb, dbf, out in items:
                e.tensor_tensor(out=acca, in0=acca, in1=accb, op=OP.add)
            e.drain()
            for x_ap, yi, acca, accb, dbf, out in items:
                e.scalar_tensor_tensor(out=out, in0=yi, scalar=float(-J), in1=acca,
                                       op0=OP.add, op1=OP.add)
            e.drain()

        def qround(e, x_ap, scale_ap, yi, acca, accb, dbf, out, J):
            e.tensor_scalar(out=yi, in0=x_ap, scalar1=scale_ap, scalar2=vs(V_OVEC),
                            op0=OP.mult, op1=OP.add)
            e.drain()
            e.tensor_scalar_add(out=yi, in0=yi, scalar1=-512.0)
            e.drain()
            e.scalar_tensor_tensor(out=dbf, in0=x_ap, scalar=scale_ap, in1=yi,
                                   op0=OP.mult, op1=OP.subtract)
            e.drain()
            ths = [j - 0.5 for j in range(-J + 1, J + 1)]
            e.tensor_scalar(out=acca, in0=dbf, scalar1=ths[0], scalar2=None,
                            op0=OP.is_ge)
            if J > 1:
                e.tensor_scalar(out=accb, in0=dbf, scalar1=ths[1], scalar2=None,
                                op0=OP.is_ge)
            e.drain()
            for i in range(2, 2 * J, 2):
                e.scalar_tensor_tensor(out=acca, in0=dbf, scalar=ths[i], in1=acca,
                                       op0=OP.is_ge, op1=OP.add)
                e.scalar_tensor_tensor(out=accb, in0=dbf, scalar=ths[i + 1], in1=accb,
                                       op0=OP.is_ge, op1=OP.add)
                e.drain()
            if J > 1:
                e.tensor_tensor(out=acca, in0=acca, in1=accb, op=OP.add)
                e.drain()
            else:
                e.scalar_tensor_tensor(out=acca, in0=dbf, scalar=ths[1], in1=acca,
                                       op0=OP.is_ge, op1=OP.add)
                e.drain()
            e.scalar_tensor_tensor(out=out, in0=yi, scalar=float(-J), in1=acca,
                                   op0=OP.add, op1=OP.add)
            e.drain()

        pl.step("vector", lambda e: e.tensor_copy(out=probe_i32, in_=sb_consts[:, 0:1]),
                waits=[("LD", ld0)])
        pl.step("vector", lambda e: e.drain())
        v_tbl = pl.step("vector", lambda e: e.tensor_scalar(out=vs(V_OVEC), in0=probe_i32,
                                                            scalar1=-0.5, scalar2=512.5,
                                                            op0=OP.mult, op1=OP.add),
                        inc="V")

        t_last_g = 0
        last_rd = {0: ("V", 0), 1: ("V", 0)}
        rot_tp = 0
        sp_v = {}
        tp_qk = 0
        sp_v_end = 0
        for gi, wd in enumerate((d_wq, d_wk, d_wv)):
            g = "qkv"[gi]
            gp_w = cast_load(wgrp, wd, D, waits=[("T", t_last_g)])
            for st in range(NST):
                half = st % 2
                psg = (PS_A if half == 0 else PS_B)[:, 0:1024]
                w = [last_rd[half]]
                if st == 0:
                    w += [("GP", gp_w), ("V", v_tbl)]
                t_mm = None
                for K in range(NDC):
                    lhs = chunk(x_fmb, NDC, K, st * 128, st * 128 + 128)
                    for n in range(2):
                        rhs = chunk(wgrp, NDC, K, n * 512, (n + 1) * 512)
                        last = (K == NDC - 1 and n == 1)
                        t_mm = pl.step("tensor", lambda e, o=psg[:, n * 512:(n + 1) * 512],
                                       l=lhs, r=rhs, s0=(K == 0), s1=(K == NDC - 1):
                                       e.matmul(o, l, r, start=s0, stop=s1),
                                       waits=w if (K == 0 and n == 0) else (),
                                       inc="T" if last else None)
                if g in ("q", "k"):
                    def reps(e, st=st):
                        e.drain()
                        for tbl, rep in ((sb_cos, cos_rep), (sb_sins, sin_rep)):
                            e.tensor_copy(out=rep[:, 0:64], in_=tbl[:, st * 64:st * 64 + 64])
                        e.drain()
                        for rep in (cos_rep, sin_rep):
                            for hh in range(1, 16):
                                e.tensor_copy(out=rep[:, hh * 64:hh * 64 + 64],
                                              in_=rep[:, 0:64])
                        e.drain()
                    pl.step("vector", reps, waits=[("T", t_mm)])
                    rb = rot_buf[st % 2]

                    def rope(e, psg=psg, rb=rb):
                        e.tensor_tensor(out=ropeA[:], in0=psg, in1=cos_rep[:], op=OP.mult)
                        sw = psg.rearrange("p (a b) -> p a b", b=2)[:, :, ::-1]
                        e.tensor_tensor(out=ropeB.rearrange("p (a b) -> p a b", b=2),
                                        in0=sw, in1=sin_rep[:].rearrange("p (a b) -> p a b", b=2),
                                        op=OP.mult)
                        e.drain()
                        e.tensor_tensor(out=rb, in0=ropeA[:], in1=ropeB, op=OP.add)
                    v_r = pl.step("vector", rope, waits=[("TP", rot_tp)], inc="V")
                    rot_tp = 1
                    last_rd[half] = ("V", v_r)
                    dstt = q_fm if g == "q" else k_fm
                    dmat(blocked(dstt, NDC, st * 128, st * 128 + 128), rb,
                         waits=[("V", v_r)])
                else:
                    vstg = rot_buf[st % 2]
                    w_v = [("T", t_mm), ("SP", sp_v.get(st - 2, 0))]
                    if st < 2:
                        w_v.append(("TP", tp_qk))
                    sc_v = pl.step(
                        "scalar", lambda e, psg=psg, vstg=vstg: e.activation(
                            out=vstg, in_=psg, func=AF.Copy),
                        waits=w_v, inc="SC")
                    sp_v[st] = dma(d_vsp.ap()[:, st * 1024:(st + 1) * 1024], vstg,
                                   waits=[("SC", sc_v)], sem="SP", eng="gpsimd")
                    last_rd[half] = ("SC", sc_v)
            if g == "k":
                tp_qk = pl.now("TP")
            if g == "v":
                sp_v_end = pl.now("SP")
            t_last_g = pl.now("T")
        tp_p1 = pl.now("TP")
        sc_p1_end = pl.now("SC")

        t_p1_end = pl.now("T")
        wmchunk = [af.at(F_W + 4096 * i, 1024) for i in range(5)]
        macw = af.at(F_W + 20480, 128)
        rl_f32 = af.at(F_W + 21504, 128)
        _p15_srcs = []
        for wd, nch, cwid in ((d_w1, NDC, C), (d_w2, NCC, D)):
            for ch in range(nch):
                for q in range(cwid // 1024):
                    _p15_srcs.append(wd.ap()[ch * 128:(ch + 1) * 128,
                                             q * 1024:(q + 1) * 1024])
        n_p15 = len(_p15_srcs)
        _p15_state = {"nd": 0, "na": 0, "ldn": {}, "sc": {}, "call": 0}

        def p15_pump(flush=False):
            st_ = _p15_state
            if not flush:
                st_["call"] += 1
                if st_["call"] % 2:
                    return
            while st_["na"] < n_p15:
                k, m = st_["nd"], st_["na"]
                if k < n_p15 and k - m <= 4:
                    if k == 0:
                        w = [("T", t_p1_end), ("TP", tp_p1)]
                    elif k >= 5:
                        w = [("V", st_["sc"][k - 5])]
                    else:
                        w = ()
                    st_["ldn"][k] = dma(wmchunk[k % 5][:], _p15_srcs[k], waits=w,
                                        sem="L2", eng="scalar")
                    st_["nd"] = k + 1
                elif m < k:
                    st_["sc"][m] = pl.step(
                        "vector", lambda e, m=m: e.reduce_sum(
                            out=macw[:, m:m + 1], in_=wmchunk[m % 5][:], axis=AX,
                            apply_absolute_value=True),
                        waits=[("L2", st_["ldn"][m])], inc="V")
                    st_["na"] = m + 1
                if not flush:
                    break

        def emit_means_finale(psum_free_v):
            p15_pump(flush=True)
            pl.step("vector", lambda e: e.drain())
            pl.step("vector", lambda e: e.reduce_sum(out=vs(V_SCR), in_=macw[:, 0:32],
                                                     axis=AX))
            v_mac = pl.step("vector", lambda e: e.reduce_sum(out=vs(V_SCR + 1),
                                                             in_=macw[:, 32:64],
                                                             axis=AX), inc="V")
            for wi in range(2):
                tot = PS_A[0:1, wi:wi + 1]
                t_tot = pl.step("tensor", lambda e, wi=wi, tot=tot: e.matmul(
                    tot, sb_ones[:, 0:1], vs(V_SCR + wi), start=True, stop=True),
                    waits=[("V", max(v_mac, psum_free_v))], inc="T")

                def mseq(e, tot=tot):
                    e.drain()
                    e.tensor_scalar(out=vs(V_SCR + 2)[0:1, :], in0=tot, scalar1=1.0 / (D * C),
                                    scalar2=1e-5, op0=OP.mult, op1=OP.max)
                    e.drain()
                    e.reciprocal(out=vs(V_SCR + 3)[0:1, :], in_=vs(V_SCR + 2)[0:1, :])
                    e.drain()
                v_m = pl.step("vector", mseq, waits=[("T", t_tot)], inc="V")
                pl.step("tensor", lambda e: e.matmul(PS_B[:, 0:1], sb_ones[0:1, 0:128],
                                                     vs(V_SCR + 2)[0:1, :], start=True,
                                                     stop=True),
                        waits=[("V", v_m)])
                t_b = pl.step("tensor", lambda e: e.matmul(
                    PS_B[:, 1:2], sb_ones[0:1, 0:128], vs(V_SCR + 3)[0:1, :],
                    start=True, stop=True), inc="T")
                mw, sc_ = (V_MW1, V_S1) if wi == 0 else (V_MW2, V_S2)
                pl.step("vector", lambda e, mw=mw: e.tensor_copy(out=vs(mw), in_=PS_B[:, 0:1]),
                        waits=[("T", t_b)])
                pl.step("vector", lambda e, sc_=sc_: e.tensor_copy(out=vs(sc_),
                                                                   in_=PS_B[:, 1:2]), inc="V")

        v_p1_end = pl.now("V")
        exp_done = {0: 0, 1: 0}
        ptm_tp = {}
        pt_rd = {0: 0, 1: 0}
        v_evac = 0
        jglob = 0
        tpsem = {0: ("sync", "TP"), 1: ("scalar", "T2")}
        vld = {}
        v_pair_free = {0: 0, 1: 0}

        def v_src(pp):
            return d_vsp.ap().rearrange("r (t c) -> r t c", c=1024)[:, :,
                                                                    pp * 128:(pp + 1) * 128]

        def v_dst(pp):
            return v_pair[pp % 2].rearrange("p (t c) -> p t c", c=128)

        for p in range(NPAIR):
            if p == 0:
                vld[0] = dma(v_dst(0), v_src(0), waits=[("SP", sp_v_end)], sem="L3")
                vld[1] = dma(v_dst(1), v_src(1), sem="L3")
            elif p + 1 < NPAIR:
                vld[p + 1] = dma(v_dst(p + 1), v_src(p + 1),
                                 waits=[("T", v_pair_free[(p + 1) % 2])], sem="L3")
            for scn in range(4):
                exp_last = {}
                tp_last = {0: ("TP", 0), 1: ("TP", 0)}
                for j in range(4):
                    st = scn * 4 + j
                    t_sc = {}
                    v_nm = {}
                    for hb in range(2):
                        PSX = PS_A if hb == 0 else PS_B
                        rows = slice(hb * 64, hb * 64 + 64)
                        lhs = q_fm[rows, p * 2048 + st * 128: p * 2048 + st * 128 + 128]
                        w = [("SC", exp_done[hb]), ("V", v_evac)]
                        if p == 0 and scn == 0 and j == 0:
                            w += [("TP", tp_p1), ("V", v_p1_end), ("SC", sc_p1_end)]
                        p15_pump()
                        for n in range(4):
                            rhs = k_fm[rows, p * 2048 + n * 512: p * 2048 + (n + 1) * 512]
                            t_sc[hb] = pl.step(
                                "tensor", lambda e, o=PSX[:, n * 512:(n + 1) * 512],
                                l=lhs, r=rhs: e.matmul(o, l, r, start=True, stop=True),
                                waits=w if n == 0 else (),
                                inc="T" if n == 3 else None)
                    for hb in range(2):
                        PSX = PS_A if hb == 0 else PS_B
                        col = hb * 4 + j
                        v_nm[hb] = pl.step("vector", lambda e, PSX=PSX, col=col: e.reduce_max(
                            out=nmbuf[:, col:col + 1], in_=PSX[:], axis=AX, negate=True),
                            waits=[("T", t_sc[hb])], inc="V")
                    slot = jglob % 3
                    jglob += 1
                    for hb in range(2):
                        PSX = PS_A if hb == 0 else PS_B
                        col = hb * 4 + j
                        pb = ptm[hb][slot]
                        teng, tsem = tpsem[hb]
                        sc_exp = pl.step("scalar", lambda e, PSX=PSX, pb=pb, col=col:
                                         e.activation(out=pb, in_=PSX[:], func=AF.Exp,
                                                      bias=nmbuf[:, col:col + 1], scale=1.0,
                                                      accum_out=lbuf[:, col:col + 1]),
                                         waits=[("V", v_nm[hb]),
                                                (tsem, ptm_tp.get((hb, slot), 0))],
                                         inc="SC")
                        exp_done[hb] = sc_exp
                        exp_last[(j, hb)] = sc_exp
                        dst = blocked(PT[hb], NTC, j * 128, j * 128 + 128)
                        tpn = dmat(dst, pb, waits=[("SC", sc_exp), ("T", pt_rd[hb])],
                                   eng=teng, sem=tsem)
                        ptm_tp[(hb, slot)] = tpn
                        tp_last[hb] = (tsem, tpn)

                rl_ps = PS_A[0:8, 1024:1152]
                t_rlt = pl.step("tensor", lambda e: e.transpose(rl_ps, lbuf[:], sb_identf),
                                waits=[("SC", exp_last[(3, 1)])], inc="T")
                def rl_recip(e):
                    e.reciprocal(out=rl_f32[0:8, :], in_=rl_ps)
                    e.drain()
                    return e.tensor_copy(out=rlrow[0:8, :], in_=rl_f32[0:8, :])
                v_rlrow = pl.step("vector", rl_recip, waits=[("T", t_rlt)], inc="V")
                for hb in range(2):
                    PSX = PS_A if hb == 0 else PS_B
                    pvs = PSX[hb * 64:hb * 64 + 64, 0:512]
                    lbs = PSX[hb * 64:hb * 64 + 64, 512:1024]
                    w = [tp_last[hb], ("V", v_rlrow), ("SC", exp_done[hb]),
                         ("L3", vld[p])]
                    for tc in range(NTC):
                        lhsv = v_pair[p % 2][:, tc * 128 + hb * 64:
                                             tc * 128 + hb * 64 + 64]
                        rhsp = PT[hb][:, tc * 512:(tc + 1) * 512]
                        pl.step("tensor", lambda e, o=pvs, l=lhsv, r=rhsp, s0=(tc == 0),
                                s1=(tc == NTC - 1): e.matmul(o, l, r, start=s0, stop=s1),
                                waits=w if tc == 0 else ())
                    t_pv = None
                    for jj in range(4):
                        sel = sb_sel[0:8, (hb * 4 + jj) * 64:(hb * 4 + jj) * 64 + 64]
                        t_pv = pl.step("tensor",
                                       lambda e, o=lbs[:, jj * 128:(jj + 1) * 128], sel=sel:
                                       e.matmul(o, sel, rlrow[0:8, :], start=True, stop=True),
                                       inc="T" if jj == 3 else None)
                    pt_rd[hb] = t_pv
                    dst = attnout[hb * 64:hb * 64 + 64,
                                  p * 2048 + scn * 512: p * 2048 + (scn + 1) * 512]
                    lrows = lbsb[hb * 64:hb * 64 + 64, :]

                    def pv_evac(e, dst=dst, pvs=pvs, lbs=lbs, lrows=lrows):
                        e.tensor_copy(out=lrows, in_=lbs)
                        e.drain()
                        e.tensor_tensor(out=dst, in0=pvs, in1=lrows, op=OP.mult)
                    v_evac = pl.step("vector", pv_evac, waits=[("T", t_pv)], inc="V")
                exp_done = {0: 0, 1: 0}
            v_pair_free[p % 2] = pl.now("T")
        emit_means_finale(pl.now("V"))
        v_attn = pl.now("V")

        gp_wo = cast_load(wo_b, d_wo, D, waits=pl.barrier())
        sp_x1 = {}
        v_lnst = {}
        v_q1d = {}
        xq_tp = {}
        x_free_v = 0
        for st in range(NST):
            psw = PS_A[:, 0:1024] if st % 2 == 0 else PS_A[:, 1024:2048]
            x_ld = dma(xst, d_xtm.ap()[st * 128:(st + 1) * 128, :],
                       waits=[("V", x_free_v if st else v_attn)])
            w = [("V", v_lnst.get(st - 2, 0))]
            if st == 0:
                w += [("GP", gp_wo), ("V", v_attn)]
            t_wo = None
            for K in range(NDC):
                lhs = attnout[:, K * 2048 + st * 128: K * 2048 + st * 128 + 128]
                for n in range(2):
                    last = (K == NDC - 1 and n == 1)
                    rhs = chunk(wo_b, NDC, K, n * 512, (n + 1) * 512)
                    t_wo = pl.step("tensor", lambda e, o=psw[:, n * 512:(n + 1) * 512], l=lhs,
                                   r=rhs, s0=(K == 0), s1=(K == NDC - 1):
                                   e.matmul(o, l, r, start=s0, stop=s1),
                                   waits=w if (K == 0 and n == 0) else (),
                                   inc="T" if last else None)

            def lnst(e, psw=psw):
                e.tensor_tensor(out=r1b[:], in0=psw, in1=xst, op=OP.add)
                e.drain()
                e.reduce_sum(out=vs(V_SCR), in_=r1b[:], axis=AX)
                e.drain()
                e.tensor_scalar_mul(out=vs(V_SCR + 1), in0=vs(V_SCR), scalar1=-1.0 / D)
                e.drain()
                e.scalar_tensor_tensor(out=scr[:], in0=r1b[:], scalar=vs(V_SCR + 1),
                                       in1=r1b[:], op0=OP.add, op1=OP.mult,
                                       accum_out=vs(V_SCR + 2))
            v_st = pl.step("vector", lnst, waits=[("T", t_wo), ("LD", x_ld)], inc="V")
            v_lnst[st] = v_st
            x_free_v = v_st
            sc_sq = pl.step("scalar", lambda e: e.activation(
                out=vs(V_SCR + 3), in_=vs(V_SCR + 2), func=AF.Sqrt,
                bias=sb_consts[:, 1:2], scale=sb_consts[:, 2:3]),
                waits=[("V", v_st)], inc="SC")

            def rstd(e):
                e.reciprocal(out=vs(V_SCR + 4), in_=vs(V_SCR + 3))
                e.drain()
                e.tensor_mul(out=vs(V_SCR + 5), in0=vs(V_SCR + 1), in1=vs(V_SCR + 4))
            v_rs = pl.step("vector", rstd, waits=[("SC", sc_sq)], inc="V")
            x1t = x1b[st % 2]

            def ln1apply(e, x1t=x1t):
                e.drain()
                e.tensor_scalar(out=x1t[:], in0=r1b[:], scalar1=vs(V_SCR + 4),
                                scalar2=vs(V_SCR + 5), op0=OP.mult, op1=OP.add)
            v_x1 = pl.step("vector", ln1apply,
                           waits=[("SP", sp_x1.get(st - 2, 0)),
                                  ("V", v_q1d.get(st - 2, 0))], inc="V")
            sp_x1[st] = dma(d_x1sp.ap()[st * 128:(st + 1) * 128, :], x1t[:],
                            waits=[("V", v_x1)], sem="SP", eng="gpsimd")
            xq = xqtm[st % 2]

            def q1(e, x1t=x1t, xq=xq):
                e.drain()
                e.reduce_max(out=vs(V_SCR), in_=x1t[:], axis=AX, apply_absolute_value=True)
                e.drain()
                e.tensor_scalar_max(out=vs(V_SCR + 1), in0=vs(V_SCR), scalar1=1e-5)
                e.drain()
                e.tensor_scalar_mul(out=vs(V_SCR + 2), in0=vs(V_SCR + 1), scalar1=1.0 / 127.0)
                e.tensor_scalar(out=vs(V_SCR + 4), in0=vs(V_SCR + 1), scalar1=vs(V_MW1),
                                scalar2=1.0 / 127.0, op0=OP.mult, op1=OP.mult)
                e.drain()
                e.reciprocal(out=vs(V_SCR + 3), in_=vs(V_SCR + 2))
                e.drain()
                qround(e, x1t[:], vs(V_SCR + 3), yi32[:], qa_w[:], qb_w[:],
                       qd_w[:], qscr[:], LQ1)
                e.tensor_scalar_mul(out=xq, in0=qscr[:], scalar1=vs(V_SCR + 4))
            v_q1 = pl.step("vector", q1, waits=[("TP", xq_tp.get(st - 2, 0))], inc="V")
            v_q1d[st] = v_q1
            xq_tp[st] = dmat(blocked(xqg_fm, NDC, st * 128, st * 128 + 128), xq,
                             waits=[("V", v_q1)])
        tp_xq = pl.now("TP")

        def ternarize(wd, nch, cw, tern, scslot, waits0):
            v_done = 0
            for ch in range(nch):
                w = list(waits0) if ch == 0 else []
                w.append(("V", v_done))
                ldn = dma(wstage[:, 0:cw], wd.ap()[ch * 128:(ch + 1) * 128, :], waits=w)
                dstt = chunk(tern, nch, ch, 0, cw)

                def tn(e, cw=cw, dstt=dstt, scslot=scslot):
                    e.drain()
                    scrs = [(wscr_i32[:], qa_w[:], qb_w[:], qd_w[:], xst),
                            (u32b[:], qa2_w[:], qb2_w[:], qd2_w[:], qscr[:])]
                    for q0 in range(0, cw, 2048):
                        items = []
                        for i, qq in enumerate((q0, q0 + 1024)):
                            if qq < cw:
                                yi, aa, bb, dd, oo = scrs[i]
                                items.append((wstage[:, qq:qq + 1024], yi, aa,
                                              bb, dd, oo))
                        qround_multi(e, items, vs(scslot), 1)
                        for i, qq in enumerate((q0, q0 + 1024)):
                            if qq < cw:
                                dch = dstt.rearrange("p (n c) -> p n c",
                                                     n=cw // 1024)[:, qq // 1024, :]
                                e.tensor_scalar(out=dch, in0=scrs[i][4], scalar1=1.0,
                                                scalar2=-1.0, op0=OP.min, op1=OP.max)
                        e.drain()
                v_done = pl.step("vector", tn, waits=[("LD", ldn)], inc="V")
            return v_done

        v_tern1 = ternarize(d_w1, NDC, C, tern1, V_S1, pl.barrier())

        v_hev = {}
        sp_h = {}
        for cc in range(NCC):
            psh = PS_A if cc % 2 == 0 else PS_B
            w = [("SC", v_hev.get(cc - 2, 0))]
            if cc == 0:
                w += [("V", v_tern1), ("TP", tp_xq)]
            t_h = None
            for K in range(NDC):
                lhs = chunk(tern1, NDC, K, cc * 128, cc * 128 + 128)
                for n in range(4):
                    last = (K == NDC - 1 and n == 3)
                    rhs = chunk(xqg_fm, NDC, K, n * 512, (n + 1) * 512)
                    t_h = pl.step("tensor", lambda e, o=psh[:, n * 512:(n + 1) * 512], l=lhs,
                                  r=rhs, s0=(K == 0), s1=(K == NDC - 1):
                                  e.matmul(o, l, r, start=s0, stop=s1),
                                  waits=w if (K == 0 and n == 0) else (),
                                  inc="T" if last else None)
            hb = hstage[cc % 2]
            v_hev[cc] = pl.step("scalar", lambda e, psh=psh, hb=hb: e.activation(
                out=hb, in_=psh[:], func=AF.Copy),
                waits=[("T", t_h), ("SP", sp_h.get(cc - 2, 0))], inc="SC")
            sp_h[cc] = dma(d_hsp.ap()[cc, :, :], hb, waits=[("SC", v_hev[cc])],
                           sem="SP", eng="gpsimd")

        v_tern2 = ternarize(d_w2, NCC, D, tern2, V_S2, pl.barrier())

        out_dma = {}
        t_conv_last = 0
        sc_silu_last = 0
        v_q2_last = 0
        t_w2_last = 0
        v_ev2 = {}
        for scn in range(4):
            lo, hi = scn * 512 - 1, scn * 512 + 513
            slo, shi = max(lo, 0), min(hi, S)
            doff = slo - lo
            src = d_hsp.ap()[:, :, slo:shi].transpose([1, 0, 2])
            dst = hsb.rearrange("p (n c) -> p n c", c=HSTRIDE)[:, :, doff:doff + (shi - slo)]
            w = [("T", t_conv_last), ("TP", pl.now("TP"))]
            if scn == 0:
                w = pl.barrier()
            ld_h = dma(dst, src, waits=w)

            def edges(e, scn=scn):
                hv = hsb.rearrange("p (n c) -> p n c", c=HSTRIDE)
                if scn == 0:
                    e.memset(hv[:, :, 0:1], 0.0)
                if scn == 3:
                    e.memset(hv[:, :, 513:514], 0.0)
                e.memset(hv[:, :, 514:HSTRIDE], 0.0)
            v_hld = pl.step("vector", edges, waits=[("LD", ld_h)], inc="V")
            sc_silu = {}
            for cc in range(NCC):
                psc = PS_A[:, (cc % 4) * 512:((cc % 4) + 1) * 512]

                def dg(e, cc=cc):
                    e.tensor_scalar_mul(out=diagb[0], in0=sb_ident,
                                        scalar1=sb_cw[:, cc * 3:cc * 3 + 1])
                    e.tensor_scalar_mul(out=diagb[1], in0=sb_ident,
                                        scalar1=sb_cw[:, cc * 3 + 1:cc * 3 + 2])
                    e.tensor_scalar_mul(out=diagb[2], in0=sb_ident,
                                        scalar1=sb_cw[:, cc * 3 + 2:cc * 3 + 3])
                v_d = pl.step("vector", dg, waits=[("V", v_hld), ("T", t_conv_last)], inc="V")
                w = [("V", v_d), ("SC", sc_silu.get(cc - 4, sc_silu_last))]
                t_cv = None
                for k in range(3):
                    rhs = hsb.rearrange("p (n c) -> p n c", c=HSTRIDE)[:, cc, k:k + 512]
                    t_cv = pl.step("tensor", lambda e, o=psc, dk=diagb[k], r=rhs, s0=(k == 0),
                                   s1=(k == 2): e.matmul(o, dk, r, start=s0, stop=s1),
                                   waits=w if k == 0 else (),
                                   inc="T" if k == 2 else None)
                t_conv_last = t_cv
                silu_dst = hsb[:, cc * HSTRIDE:cc * HSTRIDE + 512]
                sc_silu[cc] = pl.step("scalar", lambda e, psc=psc, sd=silu_dst, cc=cc:
                                      e.activation(out=sd, in_=psc, func=AF.Silu,
                                                   bias=sb_cb[:, cc:cc + 1], scale=1.0),
                                      waits=[("T", t_cv)], inc="SC")
            sc_silu_last = sc_silu[NCC - 1]
            tp_h2 = 0
            for cc in range(NCC):
                silu_src = hsb[:, cc * HSTRIDE:cc * HSTRIDE + 512]
                dst = blocked(h2tm, 4, cc * 128, (cc + 1) * 128)
                tp_h2 = dmat(dst, silu_src,
                             waits=[("SC", sc_silu[cc]), ("V", v_q2_last)])
            tp_bk = {}
            for j in range(4):
                st = scn * 4 + j
                h2 = h2tm[:, j * C:(j + 1) * C]

                def q2(e, h2=h2, j=j):
                    e.drain()
                    e.reduce_max(out=vs(V_SCR), in_=h2, axis=AX, apply_absolute_value=True)
                    e.drain()
                    e.tensor_scalar_max(out=vs(V_SCR + 1), in0=vs(V_SCR), scalar1=1e-5)
                    e.drain()
                    e.tensor_scalar_mul(out=vs(V_SCR + 2), in0=vs(V_SCR + 1), scalar1=1.0 / 127.0)
                    e.tensor_scalar(out=g3v[:, j:j + 1], in0=vs(V_SCR + 1), scalar1=vs(V_MW2),
                                    scalar2=1.0 / 127.0, op0=OP.mult, op1=OP.mult)
                    e.drain()
                    e.reciprocal(out=vs(V_SCR + 3), in_=vs(V_SCR + 2))
                    e.drain()
                    for q0 in range(0, C, 2048):
                        qround_multi(e, [
                            (h2[:, q0:q0 + 1024], u32[:], qa_c[:], qb_c[:],
                             qd_c[:], h2[:, q0:q0 + 1024]),
                            (h2[:, q0 + 1024:q0 + 2048], u32b[:], qa2_c[:],
                             qb2_c[:], qd2_c[:], h2[:, q0 + 1024:q0 + 2048]),
                        ], vs(V_SCR + 3), LQ2)
                v_q2 = pl.step("vector", q2, waits=[("TP", tp_h2)], inc="V")
                v_q2_last = v_q2
                tp_bk[j] = dmat(blocked(xq2cm, NCC, 0, 128), h2,
                                waits=[("V", v_q2), ("T", t_w2_last)])
                psw = PS_B[:, 0:1024] if j % 2 == 0 else PS_B[:, 1024:2048]
                w = [("TP", tp_bk[j]), ("V", v_ev2.get((scn, j - 2), 0))]
                if scn > 0 and j < 2:
                    w.append(("V", v_ev2.get((scn - 1, j + 2), 0)))
                t_w2 = None
                for cc in range(NCC):
                    lhs = chunk(xq2cm, NCC, cc, 0, 128)
                    for n in range(2):
                        last = (cc == NCC - 1 and n == 1)
                        rhs = chunk(tern2, NCC, cc, n * 512, (n + 1) * 512)
                        t_w2 = pl.step("tensor", lambda e, o=psw[:, n * 512:(n + 1) * 512],
                                       l=lhs, r=rhs, s0=(cc == 0), s1=(cc == NCC - 1):
                                       e.matmul(o, l, r, start=s0, stop=s1),
                                       waits=w if (cc == 0 and n == 0) else (),
                                       inc="T" if last else None)
                t_w2_last = t_w2
                ld_x1 = dma(xst, d_x1sp.ap()[st * 128:(st + 1) * 128, :],
                            waits=[("V", v_ev2.get((scn, j - 1),
                                                   v_ev2.get((scn - 1, 3), 0)) if (j or scn)
                                    else v_q2)])

                def ev2(e, psw=psw, j=j):
                    e.drain()
                    e.scalar_tensor_tensor(out=r2b[:], in0=psw, scalar=g3v[:, j:j + 1],
                                           in1=xst, op0=OP.mult, op1=OP.add)
                    e.drain()
                    e.reduce_sum(out=vs(V_SCR), in_=r2b[:], axis=AX)
                    e.drain()
                    e.tensor_scalar_mul(out=vs(V_SCR + 1), in0=vs(V_SCR), scalar1=-1.0 / D)
                    e.drain()
                    e.scalar_tensor_tensor(out=scr[:], in0=r2b[:], scalar=vs(V_SCR + 1),
                                           in1=r2b[:], op0=OP.add, op1=OP.mult,
                                           accum_out=vs(V_SCR + 2))
                v_s2 = pl.step("vector", ev2, waits=[("T", t_w2), ("LD", ld_x1)], inc="V")
                v_ev2[(scn, j)] = v_s2
                sc_sq2 = pl.step("scalar", lambda e: e.activation(
                    out=vs(V_SCR + 3), in_=vs(V_SCR + 2), func=AF.Sqrt,
                    bias=sb_consts[:, 1:2], scale=sb_consts[:, 2:3]),
                    waits=[("V", v_s2)], inc="SC")

                def rstd2(e):
                    e.reciprocal(out=vs(V_SCR + 4), in_=vs(V_SCR + 3))
                    e.drain()
                    e.tensor_mul(out=vs(V_SCR + 5), in0=vs(V_SCR + 1), in1=vs(V_SCR + 4))
                v_r2 = pl.step("vector", rstd2, waits=[("SC", sc_sq2)], inc="V")
                ob = outb[j % 2]

                def ln2apply(e, ob=ob):
                    e.drain()
                    e.tensor_scalar(out=ob[:], in0=r2b[:], scalar1=vs(V_SCR + 4),
                                    scalar2=vs(V_SCR + 5), op0=OP.mult, op1=OP.add)
                sc_out = pl.step("vector", ln2apply,
                                 waits=[("OUT", out_dma.get(st - 2, 0))], inc="V")
                out_dma[st] = dma(d_y.ap()[st * 128:(st + 1) * 128, :], ob[:],
                                  waits=[("V", sc_out)], sem="OUT")

        pl.step("sync", lambda e: e.nop(), waits=[("OUT", pl.now("OUT"))])
        pl.step("gpsimd", lambda e: e.nop(), waits=[("SP", pl.now("SP"))])

        with nc.Block() as blk:
            @blk.sync
            def _(e):
                for fn in pl.q["sync"]:
                    fn(e)

            @blk.gpsimd
            def _(e):
                for fn in pl.q["gpsimd"]:
                    fn(e)

            @blk.tensor
            def _(e):
                for fn in pl.q["tensor"]:
                    fn(e)

            @blk.vector
            def _(e):
                for fn in pl.q["vector"]:
                    fn(e)

            @blk.scalar
            def _(e):
                for fn in pl.q["scalar"]:
                    fn(e)

    return nc


def _rope_tables():
    import jax
    import jax.numpy as jnp
    with jax.default_device(jax.devices("cpu")[0]):
        pos = jnp.arange(S, dtype=jnp.float32)[:, None]
        ids = jnp.arange(HD // 2, dtype=jnp.float32)
        theta = jnp.power(jnp.float32(10000.0), 2.0 * ids / HD)
        ang = pos / theta
        sn = np.asarray(jnp.sin(ang), np.float32)
        cs = np.asarray(jnp.cos(ang), np.float32)
    cosr = np.repeat(cs, 2, axis=1)
    sinr = np.repeat(sn, 2, axis=1) * np.tile(
        np.array([-1.0, 1.0], np.float32), HD // 2)
    tab = lambda a: np.ascontiguousarray(
        a.reshape(NST, 128, HD).transpose(1, 0, 2).reshape(128, NST * HD))
    return tab(cosr), tab(sinr)


def _host_inputs(L0_attn_x, L0_attn_w_qkv, L0_attn_w_out, L0_ffn_w1,
                 L0_ffn_conv_w, L0_ffn_conv_b, L0_ffn_w2):
    costab, sintab = _rope_tables()
    x = np.ascontiguousarray(np.asarray(L0_attn_x, np.float32))
    wqkv = np.asarray(L0_attn_w_qkv, np.float32)
    wout = np.asarray(L0_attn_w_out, np.float32)
    w1 = np.asarray(L0_ffn_w1, np.float32)
    w2 = np.asarray(L0_ffn_w2, np.float32)
    cw = np.asarray(L0_ffn_conv_w, np.float32)[:, 0, :]
    cb = np.asarray(L0_ffn_conv_b, np.float32)

    wr = wqkv.reshape(H, 3 * HD, D)
    shared = {
        "wqT": np.ascontiguousarray(wr[:, 0:HD].reshape(D, D).T),
        "wkT": np.ascontiguousarray(wr[:, HD:2 * HD].reshape(D, D).T),
        "wvT": np.ascontiguousarray(wr[:, 2 * HD:3 * HD].reshape(D, D).T),
        "woutT": np.ascontiguousarray(wout.T),
        "w1T": np.ascontiguousarray(w1.T),
        "w2T": np.ascontiguousarray(w2.T),
        "convw": np.ascontiguousarray(cw.reshape(NCC, 128, 3).transpose(1, 0, 2).reshape(128, 96)),
        "convb": np.ascontiguousarray(cb.reshape(NCC, 128).T),
        "costab": costab,
        "sintab": sintab,
        "identb": np.eye(128, dtype=ml_dtypes.bfloat16),
        "identf": np.eye(128, dtype=np.float32),
        "onesf": np.ones((128, 128), np.float32),
        "consts": np.zeros((128, 8), np.float32),
    }
    shared["consts"][:, 0] = 0.75
    shared["consts"][:, 1] = 1e-5
    shared["consts"][:, 2] = 1.0 / D
    sel8 = np.zeros((8, 512), ml_dtypes.bfloat16)
    for v in range(8):
        sel8[v, v * 64:(v + 1) * 64] = 1.0
    shared["sel8"] = sel8
    maps = []
    for b in range(8):
        m = dict(shared)
        m["x_fm"] = np.ascontiguousarray(x[b].T)
        m["x_tm"] = np.ascontiguousarray(x[b])
        maps.append(m)
    return maps


def kernel(**inputs):
    maps = _host_inputs(**inputs)
    nc = build_program()
    res = run_bass_kernel_spmd(nc, maps, core_ids=list(range(8)))
    return np.stack([np.asarray(res.results[b]["y"], dtype=np.float32) for b in range(8)],
                    axis=0)
